# version 01
# baseline (speedup 1.0000x reference)
import math
from collections import defaultdict
from contextlib import ExitStack

import numpy as np
import ml_dtypes

import concourse.bass as bass
import concourse.mybir as mybir
from concourse.bass_utils import run_bass_kernel_spmd

F32 = mybir.dt.float32
BF16 = mybir.dt.bfloat16
AF = mybir.ActivationFunctionType
ALU = mybir.AluOpType
AX = mybir.AxisListType
BF = ml_dtypes.bfloat16

NCORE = 8
T = 2048
D = 2048
KC = D // 128

A_HEADS, A_DH, A_CHUNK = 8, 256, 256
A_NCH = T // A_CHUNK
B_H, B_DK, B_DV, B_M, B_CHUNK = 16, 64, 128, 64, 128
B_NCH = T // B_CHUNK
E_EXP, HID = 8, 1024
LN_EPS = 1e-5
B_EPS = 1e-5


def _gammas():
    betas = np.linspace(np.log2(1.0 / 32.0), np.log2(1.0 / 512.0), A_HEADS)
    return (1.0 - np.exp2(betas)).astype(np.float64)


class Buf:
    __slots__ = ("name", "writer", "readers")

    def __init__(self, name=""):
        self.name = name
        self.writer = None
        self.readers = []


class Prog:
    ENGINES = ("pe", "act", "dve", "pool", "sp")

    def __init__(self, nc):
        self.nc = nc
        self.ops = {e: [] for e in self.ENGINES}
        self.counts = defaultdict(int)

    def emit(self, eng, fn, reads=(), writes=(), inc=None, extra_waits=()):
        waits = list(extra_waits)
        for b in reads:
            if b.writer is not None:
                waits.append(b.writer)
        for b in writes:
            if b.writer is not None:
                waits.append(b.writer)
            waits.extend(b.readers)
        wmax = {}
        for (s, v) in waits:
            if wmax.get(s, -1) < v:
                wmax[s] = v
        sem = "s_" + eng
        if inc is None:
            inc = 16 if eng in ("pool", "sp") else 1
        self.counts[sem] += inc
        tok = (sem, self.counts[sem])
        self.ops[eng].append((fn, sorted(wmax.items()), inc))
        for b in reads:
            b.readers.append(tok)
        for b in writes:
            b.writer = tok
            b.readers = []
        return tok

    def dma(self, eng, out, in_, reads=(), writes=(), **kw):
        return self.emit(eng, lambda E: E.dma_start(out=out, in_=in_, **kw),
                         reads=reads, writes=writes)

    def fence(self):
        b = Buf("fence")
        ew = [(s, v) for s, v in self.counts.items()]
        self.emit("dve", lambda E: E.engine_nop(), extra_waits=ew, writes=[b])
        return b

    def finalize(self, es):
        nc = self.nc
        handles = {}
        for e in self.ENGINES:
            handles["s_" + e] = es.enter_context(nc.semaphore("s_" + e))
        blk = es.enter_context(nc.Block())
        ops = self.ops
        counts = dict(self.counts)

        def run(engine_obj, eng_name):
            waited = defaultdict(int)
            own = "s_" + eng_name
            is_dma = eng_name in ("pool", "sp")
            for fn, waits, inc in ops[eng_name]:
                for (s, v) in waits:
                    if s == own and not is_dma:
                        continue
                    if waited[s] >= v:
                        continue
                    engine_obj.wait_ge(handles[s], v)
                    waited[s] = v
                fn(engine_obj).then_inc(handles["s_" + eng_name], inc)

        @blk.tensor
        def _(e):
            run(e, "pe")

        @blk.scalar
        def _(e):
            run(e, "act")

        @blk.vector
        def _(e):
            run(e, "dve")

        @blk.gpsimd
        def _(e):
            run(e, "pool")
            for s in sorted(counts):
                e.wait_ge(handles[s], counts[s])

        @blk.sync
        def _(e):
            run(e, "sp")


class Ctx:
    pass


ST_RING = 0
ST_IDENT = 4224
ST_ONES_ROW = 4352
ST_ONES_COL = 4480
ST_BIAS = 6144

SM_ONESF = 0
SM_AW = 8
SM_AW01 = 72
SM_KVSC = 96
SM_KVSF = 128
SM_BIASQ = 256
SM_BIASK = 272
SM_SCRATCH = 288
SM_ONESR = 832


def make_ctx(nc, p, es):
    c = Ctx()
    c.nc, c.p = nc, p
    c.xn = es.enter_context(nc.sbuf_tensor("xn", [128, KC * T], BF16))
    c.wb = es.enter_context(nc.sbuf_tensor("wb", [128, 32768], BF16))
    c.ac = es.enter_context(nc.sbuf_tensor("ac", [128, 16384], F32))
    c.st = es.enter_context(nc.sbuf_tensor("st", [128, 6144], BF16))
    c.sm = es.enter_context(nc.sbuf_tensor("sm", [128, 960], F32))
    c.ps = es.enter_context(nc.psum_tensor("ps", [128, 4096], F32))
    c.psb = [Buf(f"ps{i}") for i in range(8)]
    c.xn_b = [Buf(f"xn{i}") for i in range(KC)]
    c.wb_b = [Buf(f"wb{i}") for i in range(16)]
    c.ac_b = [Buf(f"ac{i}") for i in range(8)]
    c.st_b = [Buf(f"st{i}") for i in range(3)]
    c.misc = defaultdict(Buf)
    return c


def ac_r(c, col0, col1):
    return [c.ac_b[i] for i in range(col0 // 2048, (col1 + 2047) // 2048)]


def wb_r(c, col0, col1):
    return [c.wb_b[i] for i in range(col0 // 2048, (col1 + 2047) // 2048)]


def st_r(c, col0, col1):
    return [c.st_b[i] for i in range(col0 // 2048, (col1 + 2047) // 2048)]


def xn_r(c, col0, col1):
    return [c.xn_b[i] for i in range(col0 // 2048, (col1 + 2047) // 2048)]


def bias_row_ap(c, n, slot=0):
    return c.ac[:].bitcast(BF16)[0:1, 28672 + slot * 2048:28672 + slot * 2048 + n]


def ps_tiles(c, col0, col1):
    return [c.psb[i] for i in range(col0 // 512, (col1 + 511) // 512)]


def mm_group(c, mms, tiles, tag=""):
    p = c.p
    grp = Buf(tag)
    n = len(mms)
    for i, (fn, rd, st_, sp_) in enumerate(mms):
        wr = []
        if i == 0:
            wr += tiles
        if i == n - 1:
            wr.append(grp)
        p.emit("pe", fn, reads=rd, writes=wr)
    return grp


def emit_init(c, csm_ap, cbf_ap):
    p = c.p
    p.dma("sp", c.sm[:], csm_ap, writes=[c.misc["sm"]])
    p.dma("sp", c.st[:, ST_IDENT:ST_IDENT + 128], cbf_ap[:, 0:128], writes=[c.misc["ident"]])
    p.dma("sp", c.st[:, 4736:5760], cbf_ap[:, 128:1152],
          writes=[c.misc["esel"]])
    p.dma("sp", c.st[:, 5760:5888], cbf_ap[:, 1152:1280],
          writes=[c.misc["lt"]])
    p.emit("pool", lambda E: E.memset(c.st[0:1, ST_ONES_ROW:ST_ONES_ROW + 128], 1.0),
           writes=[c.misc["ones_row"]], inc=1)
    p.emit("pool", lambda E: E.memset(c.st[:, ST_ONES_COL:ST_ONES_COL + 1], 1.0),
           writes=[c.misc["ones_col"]], inc=1)
    p.emit("pool", lambda E: E.memset(c.st[:, 4608:4736], 0.0),
           writes=[c.misc["ones_p0"]], inc=1)
    p.emit("pool", lambda E: E.memset(c.st[0:1, 4608:4736], 1.0),
           writes=[c.misc["ones_p0"]], inc=1)



def emit_ln(c, xsrc_ap, tag="", xsrc_b=None):
    p = c.p
    onesf = c.sm[:, SM_ONESF:SM_ONESF + 1]
    smb = c.misc["sm"]
    xreg = [c.ac_b[0], c.ac_b[1]]
    sqreg = [c.ac_b[2]]
    stats_tiles = ps_tiles(c, 0, 4096)
    grp = Buf(tag + "lgrp")
    for kc in range(KC):
        ring = kc % 2
        xs = c.ac[:, ring * 2048:ring * 2048 + 2048]
        p.dma("sp", xs, xsrc_ap[kc * 128:(kc + 1) * 128, :],
              reads=([xsrc_b] if xsrc_b else []), writes=[xreg[ring]])
        sq = c.ac[:, 4096:6144]
        p.emit("act", lambda E, xs=xs, sq=sq: E.activation(out=sq, in_=xs, func=AF.Square),
               reads=[xreg[ring]], writes=sqreg)
        for tg in range(4):
            p.emit("pe", lambda E, xs=xs, tg=tg, kc=kc: E.matmul(
                c.ps[0:1, tg * 512:(tg + 1) * 512], onesf,
                xs[:, tg * 512:(tg + 1) * 512], start=(kc == 0), stop=(kc == KC - 1)),
                reads=[xreg[ring], smb],
                writes=(stats_tiles if (kc == 0 and tg == 0) else []))
            p.emit("pe", lambda E, sq=sq, tg=tg, kc=kc: E.matmul(
                c.ps[0:1, 2048 + tg * 512:2048 + (tg + 1) * 512], onesf,
                sq[:, tg * 512:(tg + 1) * 512], start=(kc == 0), stop=(kc == KC - 1)),
                reads=sqreg + [smb],
                writes=([grp] if (kc == KC - 1 and tg == 3) else []))
    m_row = c.ac[0:1, 10240:12288]
    v_row = c.ac[0:1, 12288:14336]
    mr_row = c.ac[0:1, 14336:16384]
    mreg, vreg, mrreg = [c.ac_b[5]], [c.ac_b[6]], [c.ac_b[7]]
    p.emit("dve", lambda E: E.tensor_scalar(out=m_row, in0=c.ps[0:1, 0:2048],
           scalar1=1.0 / D, scalar2=None, op0=ALU.mult),
           reads=[grp] + stats_tiles, writes=mreg)
    p.emit("dve", lambda E: E.tensor_scalar(out=v_row, in0=c.ps[0:1, 2048:4096],
           scalar1=1.0 / D, scalar2=None, op0=ALU.mult),
           reads=[grp] + stats_tiles, writes=vreg)
    p.emit("dve", lambda E: E.tensor_mul(out=mr_row, in0=m_row, in1=m_row),
           reads=mreg, writes=mrreg)
    p.emit("dve", lambda E: E.tensor_sub(out=v_row, in0=v_row, in1=mr_row),
           reads=vreg + mrreg, writes=vreg)
    p.emit("dve", lambda E: E.tensor_scalar(out=v_row, in0=v_row, scalar1=LN_EPS,
           scalar2=None, op0=ALU.add), reads=vreg, writes=vreg)
    p.emit("act", lambda E: E.activation(out=v_row, in_=v_row, func=AF.Ln),
           reads=vreg, writes=vreg)
    p.emit("act", lambda E: E.activation(out=v_row, in_=v_row, func=AF.Exp, scale=-0.5),
           reads=vreg, writes=vreg)
    p.emit("dve", lambda E: E.tensor_mul(out=mr_row, in0=m_row, in1=v_row),
           reads=mreg + vreg, writes=mrreg)
    rbc = c.ac[:, 6144:8192]
    mrbc = c.ac[:, 8192:10240]
    rbcreg, mrbcreg = [c.ac_b[3]], [c.ac_b[4]]
    bt = ps_tiles(c, 0, 4096)
    bgrp = Buf(tag + "lbg")
    onesr = c.sm[0:1, SM_ONESR:SM_ONESR + 128]
    for tg in range(4):
        p.emit("pe", lambda E, tg=tg: E.matmul(
            c.ps[:, tg * 512:(tg + 1) * 512], onesr, v_row[:, tg * 512:(tg + 1) * 512],
            start=True, stop=True), reads=vreg + [smb],
            writes=(bt if tg == 0 else []))
        p.emit("pe", lambda E, tg=tg: E.matmul(
            c.ps[:, 2048 + tg * 512:2048 + (tg + 1) * 512], onesr,
            mr_row[:, tg * 512:(tg + 1) * 512], start=True, stop=True),
            reads=mrreg + [smb], writes=([bgrp] if tg == 3 else []))
    p.emit("dve", lambda E: E.tensor_copy(out=rbc, in_=c.ps[:, 0:2048]),
           reads=[bgrp] + bt, writes=rbcreg)
    p.emit("dve", lambda E: E.tensor_copy(out=mrbc, in_=c.ps[:, 2048:4096]),
           reads=[bgrp] + bt, writes=mrbcreg)
    for kc in range(KC):
        ring = kc % 2
        xs = c.ac[:, ring * 2048:ring * 2048 + 2048]
        p.dma("sp", xs, xsrc_ap[kc * 128:(kc + 1) * 128, :],
              reads=([xsrc_b] if xsrc_b else []), writes=[xreg[ring]])
        sq = c.ac[:, 4096:6144]
        p.emit("dve", lambda E, xs=xs, sq=sq: E.tensor_mul(out=sq, in0=xs, in1=rbc),
               reads=[xreg[ring]] + rbcreg, writes=sqreg)
        p.emit("dve", lambda E, sq=sq, kc=kc: E.tensor_sub(
            out=c.xn[:, kc * T:kc * T + 2048], in0=sq, in1=mrbc),
            reads=sqreg + mrbcreg, writes=[c.xn_b[kc]])


PROJ_SLOT = [0]


def emit_proj(c, W_ap, n_out, mode, spill_ap, bias=None, act=None, scale=None, tag="", spill_buf=None):
    p = c.p
    smb = c.misc["sm"]
    bslot = PROJ_SLOT[0] % 2
    PROJ_SLOT[0] += 1
    for kc in range(KC):
        slot = c.wb[:, kc * 2048:kc * 2048 + n_out]
        p.dma("pool", slot, W_ap[kc * 128:(kc + 1) * 128, :], writes=[c.wb_b[kc]])
    brow_b = None
    if bias is not None and bias[0] == "w17":
        brow_b = c.ac_b[7]
        w17 = c.ac[:].bitcast(BF16)[:, 28672:28672 + n_out]
        p.dma("pool", w17, W_ap[2048:2176, :], writes=[brow_b])

    if mode == "fm":
        OC2 = n_out // 256
        for oc2 in range(OC2):
            mms = []
            for kc in range(KC):
                for oc in range(2):
                    for tg in range(4):
                        ti = oc * 4 + tg
                        fn = (lambda E, kc=kc, oc=oc, tg=tg, ti=ti, oc2=oc2: E.matmul(
                            c.ps[:, ti * 512:(ti + 1) * 512],
                            c.wb[:, kc * 2048 + oc2 * 256 + oc * 128:
                                    kc * 2048 + oc2 * 256 + (oc + 1) * 128],
                            c.xn[:, kc * T + tg * 512:kc * T + (tg + 1) * 512],
                            start=(kc == 0), stop=(kc == KC - 1)))
                        mms.append((fn, [c.wb_b[kc], c.xn_b[kc]], kc == 0, kc == KC - 1))
            grp = mm_group(c, mms, ps_tiles(c, 0, 4096), tag + f"g{oc2}")
            stg = c.st[:, 0:4096]
            sreg = st_r(c, 0, 4096)
            for oc in range(2):
                for tg in range(4):
                    ti = oc * 4 + tg
                    rd = [grp] + ps_tiles(c, ti * 512, (ti + 1) * 512) + [smb]
                    if bias is not None:
                        bcol = c.sm[:, bias[1] + oc2 * 2 + oc:bias[1] + oc2 * 2 + oc + 1]
                        p.emit("act", lambda E, ti=ti, bcol=bcol, stg=stg: E.activation(
                            out=stg[:, ti * 512:(ti + 1) * 512],
                            in_=c.ps[:, ti * 512:(ti + 1) * 512],
                            func=AF.Identity, bias=bcol), reads=rd, writes=sreg)
                    elif scale is not None:
                        p.emit("act", lambda E, ti=ti, stg=stg: E.activation(
                            out=stg[:, ti * 512:(ti + 1) * 512],
                            in_=c.ps[:, ti * 512:(ti + 1) * 512],
                            func=AF.Copy, scale=scale), reads=rd, writes=sreg)
                    else:
                        p.emit("act", lambda E, ti=ti, stg=stg: E.activation(
                            out=stg[:, ti * 512:(ti + 1) * 512],
                            in_=c.ps[:, ti * 512:(ti + 1) * 512],
                            func=AF.Copy), reads=rd, writes=sreg)
            p.dma("sp", spill_ap[oc2 * 2:oc2 * 2 + 2].rearrange("o p t -> p o t"),
                  stg.rearrange("p (o t) -> p o t", o=2), reads=sreg,
                  writes=[spill_buf or Buf("spl")])
    else:
        NG = n_out // 512
        for tt in range(16):
            par = tt % 2
            base = par * 4
            mms = []
            for kc in range(KC):
                for og in range(NG):
                    ti = base + og
                    fn = (lambda E, kc=kc, og=og, ti=ti, tt=tt: E.matmul(
                        c.ps[:, ti * 512:(ti + 1) * 512],
                        c.xn[:, kc * T + tt * 128:kc * T + (tt + 1) * 128],
                        c.wb[:, kc * 2048 + og * 512:kc * 2048 + (og + 1) * 512],
                        start=(kc == 0),
                        stop=(kc == KC - 1 and bias is None)))
                    mms.append((fn, [c.wb_b[kc], c.xn_b[kc]], kc == 0, False))
            if bias is not None:
                w17 = c.ac[:].bitcast(BF16)[:, 28672:28672 + n_out]
                for og in range(NG):
                    ti = base + og
                    fn = (lambda E, og=og, ti=ti, w17=w17: E.matmul(
                        c.ps[:, ti * 512:(ti + 1) * 512],
                        c.st[:, 4608:4736],
                        w17[:, og * 512:(og + 1) * 512],
                        start=False, stop=True))
                    mms.append((fn, [brow_b, c.misc["ones_p0"]], False, True))
            grp = mm_group(c, mms, ps_tiles(c, base * 512, (base + NG) * 512), tag + f"t{tt}")
            stg = c.st[:, par * 2048:par * 2048 + 2048]
            sreg = st_r(c, par * 2048, par * 2048 + 2048)
            for og in range(NG):
                ti = base + og
                src_ = c.ps[:, ti * 512:(ti + 1) * 512]
                dst = stg[:, og * 512:(og + 1) * 512]
                rd = [grp] + ps_tiles(c, ti * 512, (ti + 1) * 512)
                if act == "silu":
                    p.emit("act", lambda E, src_=src_, dst=dst: E.activation(
                        out=dst, in_=src_, func=AF.Silu), reads=rd, writes=sreg)
                elif act == "expclip":
                    p.emit("dve", lambda E, src_=src_: E.tensor_scalar(
                        out=src_, in0=src_, scalar1=-32.0, scalar2=32.0,
                        op0=ALU.max, op1=ALU.min), reads=rd,
                        writes=ps_tiles(c, ti * 512, (ti + 1) * 512))
                    p.emit("act", lambda E, src_=src_, dst=dst: E.activation(
                        out=dst, in_=src_, func=AF.Exp),
                        reads=[grp] + ps_tiles(c, ti * 512, (ti + 1) * 512), writes=sreg)
                elif scale is not None:
                    p.emit("act", lambda E, src_=src_, dst=dst: E.activation(
                        out=dst, in_=src_, func=AF.Copy, scale=scale),
                        reads=rd, writes=sreg)
                else:
                    p.emit("act", lambda E, src_=src_, dst=dst: E.activation(
                        out=dst, in_=src_, func=AF.Copy), reads=rd, writes=sreg)
            p.dma("sp", spill_ap[tt], stg[:, 0:n_out], reads=sreg,
                  writes=[spill_buf or Buf("spl")])


def emit_summary_a(c, ktm_ap, vtm_ap, sloc_ap):
    p = c.p
    smb = c.misc["sm"]
    kb = [c.xn_b[0], c.xn_b[1]]
    vb = [c.xn_b[2], c.xn_b[3]]
    scb = c.wb_b[0]
    tiles = c.psb
    grp = Buf("sagrp")
    for tt in range(16):
        ring = tt % 2
        kslot = c.xn[:, ring * 2048:(ring + 1) * 2048]
        vslot = c.xn[:, 4096 + ring * 2048:4096 + (ring + 1) * 2048]
        p.dma("sp", kslot, ktm_ap[tt], writes=[kb[ring]])
        p.dma("sp", vslot, vtm_ap[tt], writes=[vb[ring]])
        ksc = c.wb[:, 0:2048]
        for h in range(A_HEADS):
            p.emit("dve", lambda E, h=h, kslot=kslot, tt=tt: E.tensor_scalar(
                out=ksc[:, h * 256:(h + 1) * 256],
                in0=kslot[:, h * 256:(h + 1) * 256],
                scalar1=c.sm[:, SM_KVSF + h * 16 + tt:SM_KVSF + h * 16 + tt + 1],
                scalar2=None, op0=ALU.mult),
                reads=[kb[ring], smb], writes=[scb])
        for h in range(A_HEADS):
            for dkc in range(2):
                ti = h * 2 + dkc
                wr = []
                if tt == 0 and ti == 0:
                    wr += tiles
                if tt == 15 and ti == 15:
                    wr.append(grp)
                p.emit("pe", lambda E, h=h, dkc=dkc, ti=ti, ksc=ksc, vslot=vslot, tt=tt: E.matmul(
                    c.ps[:, ti * 256:(ti + 1) * 256],
                    ksc[:, h * 256 + dkc * 128:h * 256 + (dkc + 1) * 128],
                    vslot[:, h * 256:(h + 1) * 256],
                    start=(tt == 0), stop=(tt == 15)),
                    reads=[scb, vb[ring]], writes=wr)
    eb = Buf("saev")
    p.emit("dve", lambda E: E.tensor_copy(out=c.ac[:, 0:4096], in_=c.ps[:, 0:4096]),
           reads=[grp] + tiles, writes=ac_r(c, 0, 4096) + [eb])
    p.dma("sp", sloc_ap.rearrange("(x p) d -> p x d", p=128),
          c.ac[:, 0:4096].rearrange("p (x d) -> p x d", x=16),
          reads=[eb], writes=[Buf("out")])


def build_l1():
    nc = bass.Bass("TRN2", target_bir_lowering=False, debug=False)
    xT = nc.dram_tensor("xT", [D, T], F32, kind="ExternalInput").ap()
    Wk = nc.dram_tensor("Wk", [2176, D], F32, kind="ExternalInput").ap()
    Wv = nc.dram_tensor("Wv", [2176, D], F32, kind="ExternalInput").ap()
    csm = nc.dram_tensor("csm", [128, 960], F32, kind="ExternalInput").ap()
    cbf = nc.dram_tensor("cbf", [128, 1280], BF16, kind="ExternalInput").ap()
    sloc = nc.dram_tensor("sloc", [2048, 256], F32, kind="ExternalOutput").ap()
    ktm = nc.dram_tensor("l1ktm", [16, 128, D], BF16, kind="Internal").ap()
    vtm = nc.dram_tensor("l1vtm", [16, 128, D], BF16, kind="Internal").ap()
    es = ExitStack()
    with es:
        p = Prog(nc)
        c = make_ctx(nc, p, es)
        emit_init(c, csm, cbf)
        emit_ln(c, xT, "l1")
        emit_proj(c, Wk, D, "tm", ktm, bias=("w17",), tag="k")
        emit_proj(c, Wv, D, "tm", vtm, bias=("w17",), tag="v")
        emit_summary_a(c, ktm, vtm, sloc)
        p.finalize(es)
    return nc


def emit_combine_a(c, sall_ap):
    p = c.p
    smb = c.misc["sm"]
    S = c.ac[:, 4096:8192]
    sreg = ac_r(c, 4096, 8192)
    p.emit("pool", lambda E: E.memset(S, 0.0), writes=sreg, inc=1)
    stage = c.ac[:, 0:4096]
    streg = ac_r(c, 0, 4096)
    for j in range(8):
        p.dma("sp", stage.rearrange("p (x d) -> p x d", x=16),
              sall_ap[j].rearrange("(x p) d -> p x d", p=128), writes=streg)
        for h in range(A_HEADS):
            p.emit("dve", lambda E, j=j, h=h: E.scalar_tensor_tensor(
                out=S[:, h * 512:(h + 1) * 512], in0=stage[:, h * 512:(h + 1) * 512],
                scalar=c.sm[:, SM_AW + h * 8 + j:SM_AW + h * 8 + j + 1],
                in1=S[:, h * 512:(h + 1) * 512], op0=ALU.mult, op1=ALU.add),
                reads=streg + [smb] + sreg, writes=sreg)


def emit_scan_a(c, aq, akfm, aktm, av, cbigA_ap, xsrc_ap, xdst_ap, xsrc_b, xdst_b):
    p = c.p
    smb = c.misc["sm"]
    gam = _gammas()
    mk_reg = ac_r(c, 8192, 14336)
    p.dma("sp", c.ac[:, 8192:14336], cbigA_ap, writes=mk_reg)
    S = c.ac[:, 4096:8192]
    sreg = ac_r(c, 4096, 8192)
    Sbf = c.wb[:, 0:4096]
    sbf_reg = wb_r(c, 0, 4096)
    scr_reg = wb_r(c, 4096, 6144)
    KQm = c.wb[:, 4096:4608]
    qsc = c.wb[:, 4608:5120]
    ksc = c.wb[:, 5120:5632]
    xacc = c.ac[:, 0:4096]
    xreg = ac_r(c, 0, 4096)
    p.emit("dve", lambda E: E.tensor_copy(out=Sbf, in_=S), reads=sreg, writes=sbf_reg)
    qb = [Buf("scq0"), Buf("scq1")]
    kfb = [Buf("sckf0"), Buf("sckf1")]
    ktb = [Buf("sckt0"), Buf("sckt1")]
    vb = [Buf("scv0"), Buf("scv1")]
    for n in range(A_NCH):
        r = n % 2
        qs = c.xn[:, r * 4096:r * 4096 + 4096]
        kfs = c.xn[:, 8192 + r * 4096:8192 + r * 4096 + 4096]
        kts = c.xn[:, 16384 + r * 4096:16384 + r * 4096 + 4096]
        vs = c.xn[:, 24576 + r * 4096:24576 + r * 4096 + 4096]
        p.dma("sp", qs.rearrange("p (x d) -> p x d", x=16),
              aq.rearrange("x p d -> p x d")[:, :, n * 256:(n + 1) * 256],
              reads=[c.misc["d_aq"]],
              writes=[qb[r]] + xn_r(c, r * 4096, r * 4096 + 4096))
        p.dma("sp", kfs.rearrange("p (x d) -> p x d", x=16),
              akfm.rearrange("x p d -> p x d")[:, :, n * 256:(n + 1) * 256],
              reads=[c.misc["d_akfm"]],
              writes=[kfb[r]] + xn_r(c, 8192 + r * 4096, 8192 + r * 4096 + 4096))
        p.dma("sp", kts.rearrange("p (x d) -> p x d", x=2),
              aktm[2 * n:2 * n + 2].rearrange("x p d -> p x d"),
              reads=[c.misc["d_aktm"]],
              writes=[ktb[r]] + xn_r(c, 16384 + r * 4096, 16384 + r * 4096 + 4096))
        p.dma("sp", vs.rearrange("p (x d) -> p x d", x=2),
              av[2 * n:2 * n + 2].rearrange("x p d -> p x d"),
              reads=[c.misc["d_av"]],
              writes=[vb[r]] + xn_r(c, 24576 + r * 4096, 24576 + r * 4096 + 4096))
        p.dma("sp", xacc.rearrange("p (x d) -> p x d", x=16),
              xsrc_ap.rearrange("(x p) t -> p x t", p=128)[:, :, n * 256:(n + 1) * 256],
              reads=[xsrc_b], writes=xreg)
        for h in range(A_HEADS):
            pb = (h % 2) * 2048
            kq_t = ps_tiles(c, pb, pb + 512)
            out_t = ps_tiles(c, pb + 512, pb + 1024)
            st_t = ps_tiles(c, pb + 1024, pb + 1536)
            mms = []
            for sc in range(2):
                for dkc in range(2):
                    x = h * 2 + dkc
                    fn = (lambda E, sc=sc, x=x, dkc=dkc, kfs=kfs, qs=qs, pb=pb: E.matmul(
                        c.ps[:, pb + sc * 256:pb + sc * 256 + 256],
                        kfs[:, x * 256 + sc * 128:x * 256 + sc * 128 + 128],
                        qs[:, x * 256:x * 256 + 256],
                        start=(dkc == 0), stop=(dkc == 1)))
                    mms.append((fn, [kfb[r], qb[r]], dkc == 0, dkc == 1))
            kqg = mm_group(c, mms, kq_t, f"kq{n}_{h}")
            for sc in range(2):
                p.emit("dve", lambda E, sc=sc, h=h, pb=pb: E.tensor_mul(
                    out=KQm[:, sc * 256:sc * 256 + 256],
                    in0=c.ps[:, pb + sc * 256:pb + sc * 256 + 256],
                    in1=c.ac[:, 8192 + h * 512 + sc * 256:8192 + h * 512 + sc * 256 + 256]),
                    reads=[kqg] + kq_t + mk_reg, writes=scr_reg)
            for dkc in range(2):
                x = h * 2 + dkc
                p.emit("dve", lambda E, x=x, dkc=dkc, h=h, qs=qs: E.tensor_mul(
                    out=qsc[:, dkc * 256:dkc * 256 + 256],
                    in0=qs[:, x * 256:x * 256 + 256],
                    in1=c.ac[:, 12288 + h * 256:12288 + h * 256 + 256]),
                    reads=[qb[r]] + mk_reg, writes=scr_reg)
            mms = []
            for dvc in range(2):
                for tt_s in range(2):
                    fn = (lambda E, dvc=dvc, tt_s=tt_s, h=h, vs=vs, pb=pb: E.matmul(
                        c.ps[:, pb + 512 + dvc * 256:pb + 512 + dvc * 256 + 256],
                        vs[:, tt_s * 2048 + h * 256 + dvc * 128:tt_s * 2048 + h * 256 + dvc * 128 + 128],
                        KQm[:, tt_s * 256:tt_s * 256 + 256],
                        start=(tt_s == 0), stop=False))
                    mms.append((fn, [vb[r]] + scr_reg, tt_s == 0, False))
                for dkc in range(2):
                    x = h * 2 + dkc
                    fn = (lambda E, dvc=dvc, dkc=dkc, x=x, pb=pb: E.matmul(
                        c.ps[:, pb + 512 + dvc * 256:pb + 512 + dvc * 256 + 256],
                        Sbf[:, x * 256 + dvc * 128:x * 256 + dvc * 128 + 128],
                        qsc[:, dkc * 256:dkc * 256 + 256],
                        start=False, stop=(dkc == 1)))
                    mms.append((fn, sbf_reg + scr_reg, False, dkc == 1))
            outg = mm_group(c, mms, out_t, f"oc{n}_{h}")
            for dvc in range(2):
                x = h * 2 + dvc
                p.emit("dve", lambda E, x=x, dvc=dvc, pb=pb: E.tensor_add(
                    out=xacc[:, x * 256:x * 256 + 256],
                    in0=xacc[:, x * 256:x * 256 + 256],
                    in1=c.ps[:, pb + 512 + dvc * 256:pb + 512 + dvc * 256 + 256]),
                    reads=[outg] + out_t + xreg, writes=xreg)
            for tt_s in range(2):
                p.emit("dve", lambda E, tt_s=tt_s, h=h, kts=kts: E.tensor_scalar(
                    out=ksc[:, tt_s * 256:tt_s * 256 + 256],
                    in0=kts[:, tt_s * 2048 + h * 256:tt_s * 2048 + h * 256 + 256],
                    scalar1=c.sm[:, SM_KVSC + h * 2 + tt_s:SM_KVSC + h * 2 + tt_s + 1],
                    scalar2=None, op0=ALU.mult),
                    reads=[ktb[r], smb], writes=scr_reg)
            mms = []
            for dkc in range(2):
                for tt_s in range(2):
                    fn = (lambda E, dkc=dkc, tt_s=tt_s, h=h, vs=vs, pb=pb: E.matmul(
                        c.ps[:, pb + 1024 + dkc * 256:pb + 1024 + dkc * 256 + 256],
                        ksc[:, tt_s * 256 + dkc * 128:tt_s * 256 + dkc * 128 + 128],
                        vs[:, tt_s * 2048 + h * 256:tt_s * 2048 + h * 256 + 256],
                        start=(tt_s == 0), stop=(tt_s == 1)))
                    mms.append((fn, [vb[r]] + scr_reg, tt_s == 0, tt_s == 1))
            stg_ = mm_group(c, mms, st_t, f"sd{n}_{h}")
            for dkc in range(2):
                x = h * 2 + dkc
                p.emit("dve", lambda E, x=x, dkc=dkc, h=h, pb=pb: E.scalar_tensor_tensor(
                    out=S[:, x * 256:x * 256 + 256], in0=S[:, x * 256:x * 256 + 256],
                    scalar=float(gam[h] ** A_CHUNK),
                    in1=c.ps[:, pb + 1024 + dkc * 256:pb + 1024 + dkc * 256 + 256],
                    op0=ALU.mult, op1=ALU.add),
                    reads=[stg_] + st_t + sreg, writes=sreg)
        p.dma("sp", xdst_ap.rearrange("(x p) t -> p x t", p=128)[:, :, n * 256:(n + 1) * 256],
              xacc.rearrange("p (x d) -> p x d", x=16), reads=xreg, writes=[xdst_b])
        if n < A_NCH - 1:
            p.emit("dve", lambda E: E.tensor_copy(out=Sbf, in_=S), reads=sreg,
                   writes=sbf_reg)


MO_GW = 0
MO_SBF = 128
MO_ROW = 256
MO_S8 = 2048
MO_BC = 4096
MO_U1 = 6144
MO_U3 = 14336
MO_WS = 22528
SC_SCORES = 288
SC_SIG = 416
SC_M1 = 544
SC_M2 = 560
SC_T8 = 576


def emit_moe(c, gwT_r_ap, w1T_ap, w3T_ap, w2T_ap, sw1T_ap, sw3T_ap, sw2T_ap,
             xdst_ap, xdst_b, tag=""):
    p = c.p
    smb = c.misc["sm"]
    gwb = wb_r(c, MO_GW, MO_GW + 128)
    p.dma("pool", c.wb[:, MO_GW:MO_GW + 128], gwT_r_ap, writes=gwb)
    gl_t = ps_tiles(c, 3584, 4096)
    mms = []
    for tt in range(16):
        for kc in range(KC):
            fn = (lambda E, tt=tt, kc=kc: E.matmul(
                c.ps[:, 3584 + tt * 8:3584 + tt * 8 + 8],
                c.xn[:, kc * T + tt * 128:kc * T + (tt + 1) * 128],
                c.wb[:, MO_GW + kc * 8:MO_GW + kc * 8 + 8],
                start=(kc == 0), stop=(kc == KC - 1)))
            mms.append((fn, [c.xn_b[kc]] + gwb, kc == 0, kc == KC - 1))
    glg = mm_group(c, mms, gl_t, tag + "gl")
    scores = c.sm[:, SC_SCORES:SC_SCORES + 128]
    sig = c.sm[:, SC_SIG:SC_SIG + 128]
    scb = Buf(tag + "scores")
    sigb = Buf(tag + "sig")
    p.emit("act", lambda E: E.activation(out=scores, in_=c.ps[:, 3584:3712],
           func=AF.Sigmoid), reads=[glg] + gl_t + [smb], writes=[scb])
    m1 = c.sm[:, SC_M1:SC_M1 + 16]
    m2 = c.sm[:, SC_M2:SC_M2 + 16]
    t8 = c.sm[:, SC_T8:SC_T8 + 128]
    m1b, m2b, t8b = Buf(tag + "m1"), Buf(tag + "m2"), Buf(tag + "t8")
    p.emit("dve", lambda E: E.reduce_max(out=m1, in_=scores.rearrange(
        "p (a b) -> p a b", a=16), axis=AX.X), reads=[scb], writes=[m1b])
    for tt in range(16):
        p.emit("dve", lambda E, tt=tt: E.tensor_scalar(
            out=t8[:, tt * 8:tt * 8 + 8], in0=scores[:, tt * 8:tt * 8 + 8],
            scalar1=m1[:, tt:tt + 1], scalar2=None, op0=ALU.is_equal),
            reads=[scb, m1b], writes=[t8b])
        p.emit("dve", lambda E, tt=tt: E.scalar_tensor_tensor(
            out=t8[:, tt * 8:tt * 8 + 8], in0=t8[:, tt * 8:tt * 8 + 8],
            scalar=-1e30, in1=scores[:, tt * 8:tt * 8 + 8],
            op0=ALU.mult, op1=ALU.add), reads=[t8b, scb], writes=[t8b])
    p.emit("dve", lambda E: E.reduce_max(out=m2, in_=t8.rearrange(
        "p (a b) -> p a b", a=16), axis=AX.X), reads=[t8b], writes=[m2b])
    for tt in range(16):
        p.emit("dve", lambda E, tt=tt: E.tensor_scalar(
            out=t8[:, tt * 8:tt * 8 + 8], in0=scores[:, tt * 8:tt * 8 + 8],
            scalar1=m2[:, tt:tt + 1], scalar2=None, op0=ALU.is_ge),
            reads=[scb, m2b], writes=[t8b])
    p.emit("dve", lambda E: E.tensor_mul(out=sig, in0=t8, in1=scores),
           reads=[t8b, scb], writes=[sigb])
    sigbf = c.wb[:, MO_SBF:MO_SBF + 128]
    r0 = wb_r(c, 0, 2048)
    p.emit("dve", lambda E: E.tensor_copy(out=sigbf, in_=sig), reads=[sigb], writes=r0)
    sig8 = c.wb[:, MO_S8:MO_S8 + 2048]
    r1 = wb_r(c, MO_S8, MO_S8 + 2048)
    psbf = c.ps[:].bitcast(BF16)
    for tt in range(16):
        tr_t = ps_tiles(c, 3584, 4096)
        p.emit("pe", lambda E, tt=tt: E.transpose(
            out=psbf[0:8, 7168:7168 + 128],
            in_=sigbf[:, tt * 8:tt * 8 + 8],
            identity=c.st[:, ST_IDENT:ST_IDENT + 128]),
            reads=r0 + [c.misc["ident"]], writes=tr_t)
        p.emit("dve", lambda E, tt=tt: E.tensor_copy(
            out=sig8[0:8, tt * 128:tt * 128 + 128], in_=psbf[0:8, 7168:7168 + 128]),
            reads=ps_tiles(c, 3584, 4096), writes=r1)
    acc_all = [c.ac_b[i] for i in range(8)]
    ws_ctr = [0]

    def stream_w(ap_chunk, ncols):
        nslots = (ncols + 1023) // 1024
        s = ws_ctr[0]
        if (s % 10) + nslots > 10:
            s += 10 - (s % 10)
        base = MO_WS + (s % 10) * 1024
        ws_ctr[0] = s + nslots
        sl = c.wb[:, base:base + ncols]
        bufs = wb_r(c, base, base + ncols)
        p.dma("pool", sl, ap_chunk, writes=bufs)
        return sl, bufs

    for hf in range(2):
        for e in range(-1, E_EXP):
            if e >= 0:
                bc_t = ps_tiles(c, 3072, 4096)
                bcg = Buf(tag + f"bc{hf}_{e}")
                for g in range(2):
                    p.emit("pe", lambda E, g=g, e=e, hf=hf: E.matmul(
                        c.ps[:, 3072 + g * 512:3072 + (g + 1) * 512],
                        c.st[0:8, 4736 + e * 128:4736 + (e + 1) * 128],
                        sig8[0:8, hf * 1024 + g * 512:hf * 1024 + (g + 1) * 512],
                        start=True, stop=True),
                        reads=r1 + [c.misc["esel"]],
                        writes=(bc_t if g == 0 else [bcg]))
                sbc = c.wb[:, MO_BC:MO_BC + 1024]
                bcreg = wb_r(c, MO_BC, MO_BC + 1024)
                p.emit("dve", lambda E: E.tensor_copy(out=sbc, in_=c.ps[:, 3072:4096]),
                       reads=[bcg] + bc_t, writes=bcreg)
            for (which, wap, area) in (("u1", sw1T_ap if e < 0 else w1T_ap[e], MO_U1),
                                       ("u3", sw3T_ap if e < 0 else w3T_ap[e], MO_U3)):
                areg = wb_r(c, area, area + 8192)
                for oh2 in range(2):
                    mms = []
                    wsl = [None] * KC
                    wbufs = [None] * KC
                    for kc in range(KC):
                        wsl[kc], wbufs[kc] = stream_w(wap[kc * 128:(kc + 1) * 128,
                                                         oh2 * 512:(oh2 + 1) * 512], 512)
                    for kc in range(KC):
                        for oc in range(4):
                            for tg in range(2):
                                ti = oc * 2 + tg
                                fn = (lambda E, kc=kc, oc=oc, tg=tg, ti=ti, oh2=oh2, hf=hf, w=wsl[kc]: E.matmul(
                                    c.ps[:, ti * 512:(ti + 1) * 512],
                                    w[:, oc * 128:oc * 128 + 128],
                                    c.xn[:, kc * T + hf * 1024 + tg * 512:kc * T + hf * 1024 + (tg + 1) * 512],
                                    start=(kc == 0), stop=(kc == KC - 1)))
                                mms.append((fn, [wbufs[kc][0], c.xn_b[kc]], kc == 0, kc == KC - 1))
                    grp = mm_group(c, mms, ps_tiles(c, 0, 4096), tag + f"{which}{hf}_{e}_{oh2}")
                    for oc in range(4):
                        for tg in range(2):
                            ti = oc * 2 + tg
                            ohg = oh2 * 4 + oc
                            dst = c.wb[:, area + ohg * 1024 + tg * 512:area + ohg * 1024 + (tg + 1) * 512]
                            src_ = c.ps[:, ti * 512:(ti + 1) * 512]
                            rd = [grp] + ps_tiles(c, ti * 512, (ti + 1) * 512)
                            if e < 0:
                                if which == "u1":
                                    p.emit("act", lambda E, src_=src_, dst=dst: E.activation(
                                        out=dst, in_=src_, func=AF.Silu), reads=rd, writes=areg)
                                else:
                                    p.emit("act", lambda E, src_=src_, dst=dst: E.activation(
                                        out=dst, in_=src_, func=AF.Copy), reads=rd, writes=areg)
                            else:
                                p.emit("dve", lambda E, src_=src_, dst=dst, tg=tg: E.tensor_mul(
                                    out=dst, in0=src_, in1=sbc[:, tg * 512:(tg + 1) * 512]),
                                    reads=rd + bcreg, writes=areg)
                if e >= 0 and which == "u1":
                    for ohg in range(8):
                        p.emit("act", lambda E, ohg=ohg: E.activation(
                            out=c.wb[:, MO_U1 + ohg * 1024:MO_U1 + (ohg + 1) * 1024],
                            in_=c.wb[:, MO_U1 + ohg * 1024:MO_U1 + (ohg + 1) * 1024],
                            func=AF.Silu), reads=wb_r(c, MO_U1, MO_U1 + 8192),
                            writes=wb_r(c, MO_U1, MO_U1 + 8192))
            u1reg = wb_r(c, MO_U1, MO_U1 + 8192)
            u3reg = wb_r(c, MO_U3, MO_U3 + 8192)
            p.emit("dve", lambda E: E.tensor_mul(
                out=c.wb[:, MO_U1:MO_U1 + 8192], in0=c.wb[:, MO_U1:MO_U1 + 8192],
                in1=c.wb[:, MO_U3:MO_U3 + 8192]), reads=u1reg + u3reg, writes=u1reg)
            w2ap = sw2T_ap if e < 0 else w2T_ap[e]
            for ocg in range(4):
                wsl = [None] * 8
                wbufs = [None] * 8
                for hc in range(8):
                    wsl[hc], wbufs[hc] = stream_w(w2ap[hc * 128:(hc + 1) * 128,
                                                       ocg * 512:(ocg + 1) * 512], 512)
                mms = []
                for hc in range(8):
                    for oc in range(4):
                        for tg in range(2):
                            ti = oc * 2 + tg
                            fn = (lambda E, hc=hc, oc=oc, tg=tg, ti=ti, hf=hf, w=wsl[hc]: E.matmul(
                                c.ps[:, ti * 512:(ti + 1) * 512],
                                w[:, oc * 128:oc * 128 + 128],
                                c.wb[:, MO_U1 + hc * 1024 + tg * 512:MO_U1 + hc * 1024 + (tg + 1) * 512],
                                start=(hc == 0), stop=(hc == 7)))
                            mms.append((fn, [wbufs[hc][0]] + u1reg, hc == 0, hc == 7))
                grp = mm_group(c, mms, ps_tiles(c, 0, 4096), tag + f"y{hf}_{e}_{ocg}")
                for oc in range(4):
                    for tg in range(2):
                        ti = oc * 2 + tg
                        ocglob = ocg * 4 + oc
                        dst = c.ac[:, ocglob * 1024 + tg * 512:ocglob * 1024 + (tg + 1) * 512]
                        src_ = c.ps[:, ti * 512:(ti + 1) * 512]
                        rd = [grp] + ps_tiles(c, ti * 512, (ti + 1) * 512)
                        ar = [c.ac_b[ocglob // 2]]
                        if e < 0:
                            p.emit("dve", lambda E, src_=src_, dst=dst: E.tensor_copy(
                                out=dst, in_=src_), reads=rd + ar, writes=ar)
                        else:
                            p.emit("dve", lambda E, src_=src_, dst=dst: E.tensor_add(
                                out=dst, in0=dst, in1=src_), reads=rd + ar, writes=ar)
        for oc in range(16):
            p.dma("pool", xdst_ap[oc * 128:(oc + 1) * 128, hf * 1024:(hf + 1) * 1024],
                  c.ac[:, oc * 1024:(oc + 1) * 1024], accum_op=ALU.add,
                  reads=[c.ac_b[oc // 2], xdst_b], writes=[xdst_b])


def emit_summary_b(c, bktm, bE, bv, akb_ap, avb_ap, srb_ap):
    p = c.p
    kbb = [Buf("sbk0"), Buf("sbk1")]
    ebb = [Buf("sbe0"), Buf("sbe1")]
    vbb = [Buf("sbv0"), Buf("sbv1")]
    ak_t = ps_tiles(c, 0, 1024)
    av_t = ps_tiles(c, 1024, 3072)
    sr_t = ps_tiles(c, 3072, 4096)
    grp = Buf("sbgrp")
    for tt in range(16):
        r = tt % 2
        ks = c.xn[:, r * 1024:r * 1024 + 1024]
        es = c.xn[:, 2048 + r * 1024:2048 + r * 1024 + 1024]
        vs = c.xn[:, 4096 + r * 2048:4096 + r * 2048 + 2048]
        p.dma("sp", ks, bktm[tt], reads=[c.misc["d_bktm"]],
              writes=[kbb[r]] + xn_r(c, r * 1024, r * 1024 + 1024))
        p.dma("sp", es, bE[tt], reads=[c.misc["d_bE"]],
              writes=[ebb[r]] + xn_r(c, 2048 + r * 1024, 2048 + r * 1024 + 1024))
        p.dma("sp", vs, bv[tt], reads=[c.misc["d_bv"]],
              writes=[vbb[r]] + xn_r(c, 4096 + r * 2048, 4096 + r * 2048 + 2048))
        first = (tt == 0)
        last = (tt == 15)
        for h in range(B_H):
            wr = (ak_t + av_t + sr_t) if (first and h == 0) else []
            p.emit("pe", lambda E, h=h, ks=ks, es=es, first=first, last=last: E.matmul(
                c.ps[0:64, h * 64:h * 64 + 64],
                ks[:, h * 64:h * 64 + 64], es[:, h * 64:h * 64 + 64],
                start=first, stop=last),
                reads=[kbb[r], ebb[r]], writes=wr)
            p.emit("pe", lambda E, h=h, es=es, vs=vs, first=first, last=last: E.matmul(
                c.ps[0:64, 1024 + h * 128:1024 + h * 128 + 128],
                es[:, h * 64:h * 64 + 64], vs[:, h * 128:h * 128 + 128],
                start=first, stop=last),
                reads=[ebb[r], vbb[r]], writes=[])
        for g in range(2):
            wr = [grp] if (last and g == 1) else []
            p.emit("pe", lambda E, g=g, es=es, first=first, last=last: E.matmul(
                c.ps[0:1, 3072 + g * 512:3072 + (g + 1) * 512],
                c.st[:, ST_ONES_COL:ST_ONES_COL + 1],
                es[:, g * 512:(g + 1) * 512], start=first, stop=last),
                reads=[ebb[r], c.misc["ones_col"]], writes=wr)
    eb = Buf("sbev")
    p.emit("dve", lambda E: E.tensor_copy(out=c.ac[0:64, 0:3072], in_=c.ps[0:64, 0:3072]),
           reads=[grp] + ak_t + av_t, writes=ac_r(c, 0, 3072) + [eb])
    eb2 = Buf("sbev2")
    p.emit("dve", lambda E: E.tensor_copy(out=c.ac[0:1, 3072:4096], in_=c.ps[0:1, 3072:4096]),
           reads=[grp] + sr_t, writes=[eb2])
    p.dma("sp", akb_ap, c.ac[0:64, 0:1024], reads=[eb], writes=[Buf("o1")])
    p.dma("sp", avb_ap, c.ac[0:64, 1024:3072], reads=[eb], writes=[Buf("o2")])
    p.dma("sp", srb_ap, c.ac[0:1, 3072:4096], reads=[eb2], writes=[Buf("o3")])


def build_l2():
    nc = bass.Bass("TRN2", target_bir_lowering=False, debug=False)
    DT = lambda n, s, d, k: nc.dram_tensor(n, s, d, kind=k).ap()
    xT = DT("xT", [D, T], F32, "ExternalInput")
    Wq = DT("Wq", [D, D], F32, "ExternalInput")
    Wk = DT("Wk", [2176, D], F32, "ExternalInput")
    Wv = DT("Wv", [2176, D], F32, "ExternalInput")
    sall = DT("sall", [8, 2048, 256], F32, "ExternalInput")
    cbigA = DT("cbigA", [128, 6144], F32, "ExternalInput")
    gwT_r = DT("gwT_r", [128, 128], F32, "ExternalInput")
    w1T = DT("w1T", [E_EXP, D, HID], F32, "ExternalInput")
    w3T = DT("w3T", [E_EXP, D, HID], F32, "ExternalInput")
    w2T = DT("w2T", [E_EXP, HID, D], F32, "ExternalInput")
    sw1T = DT("sw1T", [D, HID], F32, "ExternalInput")
    sw3T = DT("sw3T", [D, HID], F32, "ExternalInput")
    sw2T = DT("sw2T", [HID, D], F32, "ExternalInput")
    WkB = DT("WkB", [D, HID], F32, "ExternalInput")
    WsB = DT("WsB", [D, HID], F32, "ExternalInput")
    WvB = DT("WvB", [D, D], F32, "ExternalInput")
    csm = DT("csm", [128, 960], F32, "ExternalInput")
    cbf = DT("cbf", [128, 1280], BF16, "ExternalInput")
    xres = DT("xres", [D, T], F32, "ExternalOutput")
    akb = DT("akb", [64, 1024], F32, "ExternalOutput")
    avb = DT("avb", [64, 2048], F32, "ExternalOutput")
    srb = DT("srb", [1, 1024], F32, "ExternalOutput")
    aq = DT("aq", [16, 128, T], BF16, "Internal")
    akfm = DT("akfm", [16, 128, T], BF16, "Internal")
    aktm = DT("aktm", [16, 128, D], BF16, "Internal")
    av = DT("av", [16, 128, D], BF16, "Internal")
    bktm = DT("bktm", [16, 128, HID], BF16, "Internal")
    bE = DT("bE", [16, 128, HID], BF16, "Internal")
    bv = DT("bv", [16, 128, D], BF16, "Internal")

    es = ExitStack()
    with es:
        p = Prog(nc)
        c = make_ctx(nc, p, es)
        emit_init(c, csm, cbf)
        emit_ln(c, xT, "l2a")
        emit_proj(c, Wq, D, "fm", aq, bias=("col", SM_BIASQ), tag="q",
                  spill_buf=c.misc["d_aq"])
        emit_proj(c, Wk, D, "fm", akfm, bias=("col", SM_BIASK), tag="kf",
                  spill_buf=c.misc["d_akfm"])
        emit_proj(c, Wk, D, "tm", aktm, bias=("w17",), tag="kt",
                  spill_buf=c.misc["d_aktm"])
        emit_proj(c, Wv, D, "tm", av, bias=("w17",), tag="vt",
                  spill_buf=c.misc["d_av"])
        emit_combine_a(c, sall)
        emit_scan_a(c, aq, akfm, aktm, av, cbigA, xT, xres,
                    c.misc["d_xT"], c.misc["d_xres"])
        emit_ln(c, xres, "l2b", xsrc_b=c.misc["d_xres"])
        emit_moe(c, gwT_r, w1T, w3T, w2T, sw1T, sw3T, sw2T, xres,
                 c.misc["d_xres"], tag="m1")
        emit_ln(c, xres, "l2c", xsrc_b=c.misc["d_xres"])
        emit_proj(c, WkB, HID, "tm", bktm, tag="bk", spill_buf=c.misc["d_bktm"])
        emit_proj(c, WsB, HID, "tm", bE, act="expclip", tag="bs",
                  spill_buf=c.misc["d_bE"])
        emit_proj(c, WvB, D, "tm", bv, tag="bv", spill_buf=c.misc["d_bv"])
        emit_summary_b(c, bktm, bE, bv, akb, avb, srb)
        p.finalize(es)
    return nc


def build_dbg_fm():
    nc = bass.Bass("TRN2", target_bir_lowering=False, debug=False)
    DT = lambda n, s, d, k: nc.dram_tensor(n, s, d, kind=k).ap()
    xT = DT("xT", [D, T], F32, "ExternalInput")
    Wq = DT("Wq", [D, D], F32, "ExternalInput")
    csm = DT("csm", [128, 960], F32, "ExternalInput")
    cbf = DT("cbf", [128, 1280], BF16, "ExternalInput")
    aq = DT("aq", [16, 128, T], BF16, "ExternalOutput")
    es = ExitStack()
    with es:
        p = Prog(nc)
        c = make_ctx(nc, p, es)
        emit_init(c, csm, cbf)
        emit_ln(c, xT, "d")
        emit_proj(c, Wq, D, "fm", aq, bias=("col", SM_BIASQ), tag="q",
                  spill_buf=c.misc["d_aq"])
        p.finalize(es)
    return nc


def build_dbg_noscan():
    nc = bass.Bass("TRN2", target_bir_lowering=False, debug=False)
    DT = lambda n, s, d, k: nc.dram_tensor(n, s, d, kind=k).ap()
    xT = DT("xT", [D, T], F32, "ExternalInput")
    Wq = DT("Wq", [D, D], F32, "ExternalInput")
    Wk = DT("Wk", [2176, D], F32, "ExternalInput")
    Wv = DT("Wv", [2176, D], F32, "ExternalInput")
    csm = DT("csm", [128, 960], F32, "ExternalInput")
    cbf = DT("cbf", [128, 1280], BF16, "ExternalInput")
    aq = DT("aq", [16, 128, T], BF16, "ExternalOutput")
    akfm = DT("akfm", [16, 128, T], BF16, "ExternalOutput")
    aktm = DT("aktm", [16, 128, D], BF16, "ExternalOutput")
    av = DT("av", [16, 128, D], BF16, "ExternalOutput")
    es = ExitStack()
    with es:
        p = Prog(nc)
        c = make_ctx(nc, p, es)
        emit_init(c, csm, cbf)
        emit_ln(c, xT, "l2a")
        emit_proj(c, Wq, D, "fm", aq, bias=("col", SM_BIASQ), tag="q",
                  spill_buf=c.misc["d_aq"])
        emit_proj(c, Wk, D, "fm", akfm, bias=("col", SM_BIASK), tag="kf",
                  spill_buf=c.misc["d_akfm"])
        emit_proj(c, Wk, D, "tm", aktm, bias=("w17",), tag="kt",
                  spill_buf=c.misc["d_aktm"])
        emit_proj(c, Wv, D, "tm", av, bias=("w17",), tag="vt",
                  spill_buf=c.misc["d_av"])
        p.finalize(es)
    return nc


def build_dbg_scan():
    nc = bass.Bass("TRN2", target_bir_lowering=False, debug=False)
    DT = lambda n, s, d, k: nc.dram_tensor(n, s, d, kind=k).ap()
    xT = DT("xT", [D, T], F32, "ExternalInput")
    Wq = DT("Wq", [D, D], F32, "ExternalInput")
    Wk = DT("Wk", [2176, D], F32, "ExternalInput")
    Wv = DT("Wv", [2176, D], F32, "ExternalInput")
    sall = DT("sall", [8, 2048, 256], F32, "ExternalInput")
    cbigA = DT("cbigA", [128, 6144], F32, "ExternalInput")
    csm = DT("csm", [128, 960], F32, "ExternalInput")
    cbf = DT("cbf", [128, 1280], BF16, "ExternalInput")
    xres = DT("xres", [D, T], F32, "ExternalOutput")
    aq = DT("aq", [16, 128, T], BF16, "ExternalOutput")
    akfm = DT("akfm", [16, 128, T], BF16, "ExternalOutput")
    aktm = DT("aktm", [16, 128, D], BF16, "ExternalOutput")
    av = DT("av", [16, 128, D], BF16, "ExternalOutput")
    es = ExitStack()
    with es:
        p = Prog(nc)
        c = make_ctx(nc, p, es)
        emit_init(c, csm, cbf)
        emit_ln(c, xT, "l2a")
        emit_proj(c, Wq, D, "fm", aq, bias=("col", SM_BIASQ), tag="q",
                  spill_buf=c.misc["d_aq"])
        emit_proj(c, Wk, D, "fm", akfm, bias=("col", SM_BIASK), tag="kf",
                  spill_buf=c.misc["d_akfm"])
        emit_proj(c, Wk, D, "tm", aktm, bias=("w17",), tag="kt",
                  spill_buf=c.misc["d_aktm"])
        emit_proj(c, Wv, D, "tm", av, bias=("w17",), tag="vt",
                  spill_buf=c.misc["d_av"])
        emit_combine_a(c, sall)
        emit_scan_a(c, aq, akfm, aktm, av, cbigA, xT, xres,
                    c.misc["d_xT"], c.misc["d_xres"])
        p.finalize(es)
    return nc


def emit_combine_b(c, akall, avall, srall):
    p = c.p
    smb = c.misc["sm"]
    Ak = c.ac[0:64, 0:1024]
    Av = c.ac[0:64, 1024:3072]
    Sp = c.ac[0:1, 3072:4096]
    sreg = ac_r(c, 0, 4096)
    p.emit("pool", lambda E: E.memset(c.ac[:, 0:4096], 0.0), writes=sreg, inc=1)
    streg = ac_r(c, 4096, 8192)
    for j in range(8):
        p.dma("sp", c.ac[0:64, 4096:5120], akall[j], writes=[streg[0]])
        p.dma("sp", c.ac[0:64, 5120:7168], avall[j], writes=[streg[1]])
        p.dma("sp", c.ac[0:1, 7168:8192], srall[j], writes=[Buf(f"cbs{j}")] + [streg[1]])
        awc = c.sm[0:64, SM_AW01 + j:SM_AW01 + j + 1]
        p.emit("dve", lambda E, j=j, awc=awc: E.scalar_tensor_tensor(
            out=Ak, in0=c.ac[0:64, 4096:5120], scalar=awc, in1=Ak,
            op0=ALU.mult, op1=ALU.add), reads=streg + [smb] + sreg, writes=sreg)
        p.emit("dve", lambda E, j=j, awc=awc: E.scalar_tensor_tensor(
            out=Av, in0=c.ac[0:64, 5120:7168], scalar=awc, in1=Av,
            op0=ALU.mult, op1=ALU.add), reads=streg + [smb] + sreg, writes=sreg)
        p.emit("dve", lambda E, j=j, awc=awc: E.scalar_tensor_tensor(
            out=Sp, in0=c.ac[0:1, 7168:8192], scalar=awc[0:1, :], in1=Sp,
            op0=ALU.mult, op1=ALU.add), reads=streg + [smb] + sreg, writes=sreg)


SCANB_STEPS = [99]
SCANB_CHUNKS = [B_NCH]


def emit_scan_b(c, bq, bkfm, bktm, bE, bv, bg, cbigB_ap, gnw_ap, bo, tag="sb"):
    STEPS = SCANB_STEPS[0]
    NCHS = SCANB_CHUNKS[0]
    p = c.p
    nc = c.nc
    smb = c.misc["sm"]
    p.dma("sp", c.ac[:, 4096:6144], gnw_ap, writes=ac_r(c, 4096, 6144))
    p.dma("sp", c.ac[:, 6144:8192], cbigB_ap, writes=ac_r(c, 6144, 8192))
    mkreg = ac_r(c, 4096, 8448)
    sreg = ac_r(c, 0, 4096)
    wbst = wb_r(c, 0, 4096)
    p.emit("dve", lambda E: E.tensor_copy(
           out=c.wb[0:64, 0:1024], in_=c.ac[0:64, 0:1024]), reads=sreg, writes=wbst)
    p.emit("dve", lambda E: E.tensor_copy(out=c.wb[0:64, 1024:3072],
           in_=c.ac[0:64, 1024:3072]), reads=sreg, writes=wbst)
    p.emit("dve", lambda E: E.tensor_copy(out=c.wb[0:1, 3072:4096],
           in_=c.ac[0:1, 3072:4096]), reads=sreg, writes=wbst)
    psbf = c.ps[:].bitcast(BF16)
    scr2 = wb_r(c, 4096, 8192)
    scr3 = wb_r(c, 8192, 12288)
    scr4 = wb_r(c, 12288, 13312)
    strm = [[Buf(f"{tag}s{i}{r}") for r in range(2)] for i in range(6)]
    for n in range(NCHS):
        r = n % 2
        qs = c.xn[0:64, r * 2048:r * 2048 + 2048]
        kfs = c.xn[0:64, 4096 + r * 2048:4096 + r * 2048 + 2048]
        kts = c.xn[:, 8192 + r * 1024:8192 + r * 1024 + 1024]
        es_ = c.xn[:, 10240 + r * 1024:10240 + r * 1024 + 1024]
        vs = c.xn[:, 12288 + r * 2048:12288 + r * 2048 + 2048]
        gs = c.xn[:, 16384 + r * 2048:16384 + r * 2048 + 2048]
        p.dma("sp", qs.rearrange("p (y d) -> p y d", y=16),
              bq.rearrange("x (s p) d -> p (x s) d", s=2)[:, :, n * 128:(n + 1) * 128],
              reads=[c.misc["d_bq"]], writes=[strm[0][r]] + xn_r(c, r * 2048, r * 2048 + 2048))
        p.dma("sp", kfs.rearrange("p (y d) -> p y d", y=16),
              bkfm.rearrange("x (s p) d -> p (x s) d", s=2)[:, :, n * 128:(n + 1) * 128],
              reads=[c.misc["d_bkfm"]], writes=[strm[1][r]] + xn_r(c, 4096 + r * 2048, 4096 + r * 2048 + 2048))
        p.dma("sp", kts, bktm[n], reads=[c.misc["d_bktm"]],
              writes=[strm[2][r]] + xn_r(c, 8192 + r * 1024, 8192 + r * 1024 + 1024))
        p.dma("sp", es_, bE[n], reads=[c.misc["d_bE"]],
              writes=[strm[3][r]] + xn_r(c, 10240 + r * 1024, 10240 + r * 1024 + 1024))
        p.dma("sp", vs, bv[n], reads=[c.misc["d_bv"]],
              writes=[strm[4][r]] + xn_r(c, 12288 + r * 2048, 12288 + r * 2048 + 2048))
        p.dma("sp", gs, bg[n], reads=[c.misc["d_bg"]],
              writes=[strm[5][r]] + xn_r(c, 16384 + r * 2048, 16384 + r * 2048 + 2048))
        if STEPS < 1:
            continue
        mms = []
        for h in range(B_H):
            fn = (lambda E, h=h, kfs=kfs, qs=qs: E.matmul(
                c.ps[:, h * 128:(h + 1) * 128],
                kfs[:, h * 128:(h + 1) * 128],
                qs[:, h * 128:(h + 1) * 128],
                start=True, stop=True))
            mms.append((fn, [strm[0][r], strm[1][r]], True, True))
        kqg = mm_group(c, mms, ps_tiles(c, 0, 2048), tag + f"kq{n}")
        KQm = c.wb[:, 4096:6144]
        p.emit("dve", lambda E: E.tensor_mul(out=KQm, in0=c.ps[:, 0:2048],
               in1=c.ac[:, 6144:8192]), reads=[kqg] + ps_tiles(c, 0, 2048) + mkreg,
               writes=scr2)
        if STEPS < 2:
            continue
        mms = []
        for g in range(2):
            fn = (lambda E, g=g, es_=es_: E.matmul(
                c.ps[:, 2048 + g * 512:2048 + (g + 1) * 512],
                c.st[:, 5760:5888],
                es_[:, g * 512:(g + 1) * 512], start=True, stop=False))
            mms.append((fn, [strm[3][r], c.misc["lt"]], g == 0, False))
        for g in range(2):
            fn = (lambda E, g=g: E.matmul(
                c.ps[:, 2048 + g * 512:2048 + (g + 1) * 512],
                c.st[0:1, ST_ONES_ROW:ST_ONES_ROW + 128],
                c.wb[0:1, 3072 + g * 512:3072 + (g + 1) * 512],
                start=False, stop=True))
            mms.append((fn, wbst + [c.misc["ones_row"]], False, True))
        scg = mm_group(c, mms, ps_tiles(c, 2048, 3072), tag + f"sc{n}")
        mms = []
        for g in range(2):
            fn = (lambda E, g=g, es_=es_: E.matmul(
                c.ps[0:1, 3072 + g * 512:3072 + (g + 1) * 512],
                c.st[:, ST_ONES_COL:ST_ONES_COL + 1],
                es_[:, g * 512:(g + 1) * 512], start=True, stop=True))
            mms.append((fn, [strm[3][r], c.misc["ones_col"]], True, True))
        csg = mm_group(c, mms, ps_tiles(c, 3072, 4096), tag + f"cs{n}")
        Sinv = c.ac[:, 12288:13312]
        sinvreg = ac_r(c, 12288, 13312)
        p.emit("dve", lambda E: E.reciprocal(out=Sinv, in_=c.ps[:, 2048:3072]),
               reads=[scg] + ps_tiles(c, 2048, 3072), writes=sinvreg)
        if STEPS < 3:
            continue
        mms = []
        for h in range(B_H):
            fn = (lambda E, h=h, es_=es_: E.matmul(
                c.ps[:, h * 64:(h + 1) * 64],
                KQm[:, h * 128:(h + 1) * 128],
                es_[:, h * 64:(h + 1) * 64], start=True, stop=False))
            mms.append((fn, scr2 + [strm[3][r]], h == 0, False))
            fn2 = (lambda E, h=h, qs=qs: E.matmul(
                c.ps[:, h * 64:(h + 1) * 64],
                qs[:, h * 128:(h + 1) * 128],
                c.wb[0:64, h * 64:h * 64 + 64], start=False, stop=True))
            mms.append((fn2, [strm[0][r]] + wbst, False, True))
        okg = mm_group(c, mms, ps_tiles(c, 0, 1024), tag + f"ok{n}")
        okb = c.ac[:, 13312:14336]
        okreg = ac_r(c, 13312, 14336)
        p.emit("dve", lambda E: E.tensor_mul(out=okb, in0=c.ps[:, 0:1024], in1=Sinv),
               reads=[okg] + ps_tiles(c, 0, 1024) + sinvreg, writes=okreg)
        if STEPS < 4:
            continue
        nm = c.ac[:, 8320:8336]
        rs = c.ac[:, 8336:8352]
        streg = ac_r(c, 8320, 8352)
        p.emit("dve", lambda E: E.reduce_max(out=nm, in_=okb.rearrange(
            "p (a b) -> p a b", a=16), axis=AX.X, negate=True),
            reads=okreg, writes=streg)
        for h in range(B_H):
            p.emit("act", lambda E, h=h: E.activation(
                out=okb[:, h * 64:(h + 1) * 64], in_=okb[:, h * 64:(h + 1) * 64],
                func=AF.Exp, bias=nm[:, h:h + 1]), reads=okreg + streg, writes=okreg)
        p.emit("dve", lambda E: E.reduce_sum(out=rs, in_=okb.rearrange(
            "p (a b) -> p a b", a=16), axis=AX.X), reads=okreg, writes=streg)
        p.emit("dve", lambda E: E.reciprocal(out=rs, in_=rs), reads=streg, writes=streg)
        for h in range(B_H):
            p.emit("dve", lambda E, h=h: E.scalar_tensor_tensor(
                out=okb[:, h * 64:(h + 1) * 64], in0=okb[:, h * 64:(h + 1) * 64],
                scalar=rs[:, h:h + 1], in1=Sinv[:, h * 64:(h + 1) * 64],
                op0=ALU.mult, op1=ALU.mult), reads=okreg + streg + sinvreg,
                writes=okreg)
        Wbf = c.wb[:, 12288:13312]
        p.emit("dve", lambda E: E.tensor_copy(out=Wbf, in_=okb), reads=okreg, writes=scr4)
        if STEPS < 5:
            continue
        ET = c.wb[0:64, 8192:10240]
        WT = c.wb[0:64, 10240:12288]
        trt = ps_tiles(c, 1024, 2048)
        etdone = Buf(tag + f"et{n}")
        for h in range(B_H):
            wr = list(trt) if h == 0 else []
            if h == B_H - 1:
                wr.append(etdone)
            p.emit("pe", lambda E, h=h, es_=es_: E.transpose(
                out=psbf[0:64, 2048 + h * 128:2048 + (h + 1) * 128],
                in_=es_[:, h * 64:(h + 1) * 64],
                identity=c.st[:, ST_IDENT:ST_IDENT + 128]),
                reads=[strm[3][r], c.misc["ident"]], writes=wr)
        p.emit("dve", lambda E: E.tensor_copy(out=ET, in_=psbf[0:64, 2048:4096]),
               reads=[etdone] + trt, writes=scr3)
        wtdone = Buf(tag + f"wt{n}")
        for h in range(B_H):
            wr = list(trt) if h == 0 else []
            if h == B_H - 1:
                wr.append(wtdone)
            p.emit("pe", lambda E, h=h: E.transpose(
                out=psbf[0:64, 2048 + h * 128:2048 + (h + 1) * 128],
                in_=Wbf[:, h * 64:(h + 1) * 64],
                identity=c.st[:, ST_IDENT:ST_IDENT + 128]),
                reads=scr4 + [c.misc["ident"]], writes=wr)
        p.emit("dve", lambda E: E.tensor_copy(out=WT, in_=psbf[0:64, 2048:4096]),
               reads=[wtdone] + trt, writes=scr3)
        if STEPS < 6:
            continue
        mms = []
        for h in range(B_H):
            fn = (lambda E, h=h: E.matmul(
                c.ps[:, h * 128:(h + 1) * 128],
                ET[:, h * 128:(h + 1) * 128],
                WT[:, h * 128:(h + 1) * 128], start=True, stop=True))
            mms.append((fn, scr3, True, True))
        ptg = mm_group(c, mms, ps_tiles(c, 0, 2048), tag + f"pt{n}")
        PTm = c.wb[:, 6144:8192]
        p.emit("dve", lambda E: E.tensor_mul(out=PTm, in0=c.ps[:, 0:2048],
               in1=c.ac[:, 6144:8192]), reads=[ptg] + ps_tiles(c, 0, 2048) + mkreg,
               writes=scr2)
        if STEPS < 7:
            continue
        mms = []
        for h in range(B_H):
            fn = (lambda E, h=h, vs=vs: E.matmul(
                c.ps[:, h * 128:(h + 1) * 128],
                PTm[:, h * 128:(h + 1) * 128],
                vs[:, h * 128:(h + 1) * 128], start=True, stop=False))
            mms.append((fn, scr2 + [strm[4][r]], h == 0, False))
            fn2 = (lambda E, h=h: E.matmul(
                c.ps[:, h * 128:(h + 1) * 128],
                WT[:, h * 128:(h + 1) * 128],
                c.wb[0:64, 1024 + h * 128:1024 + (h + 1) * 128],
                start=False, stop=True))
            mms.append((fn2, scr3 + wbst, False, True))
        og = mm_group(c, mms, ps_tiles(c, 0, 2048), tag + f"o{n}")
        if STEPS < 8:
            continue
        o2 = c.ac[:, 10240:12288]
        o2reg = ac_r(c, 10240, 12288)
        p.emit("act", lambda E: E.activation(out=o2, in_=c.ps[:, 0:2048],
               func=AF.Square), reads=[og] + ps_tiles(c, 0, 2048), writes=o2reg)
        rstd = c.ac[:, 8352:8368]
        p.emit("dve", lambda E: E.reduce_sum(out=rstd, in_=o2.rearrange(
            "p (a b) -> p a b", a=16), axis=AX.X), reads=o2reg, writes=streg)
        p.emit("dve", lambda E: E.tensor_scalar(out=rstd, in0=rstd,
               scalar1=1.0 / B_DV, scalar2=B_EPS, op0=ALU.mult, op1=ALU.add),
               reads=streg, writes=streg)
        p.emit("dve", lambda E: E.reciprocal(out=rstd, in_=rstd), reads=streg, writes=streg)
        p.emit("act", lambda E: E.activation(out=rstd, in_=rstd, func=AF.Sqrt),
               reads=streg, writes=streg)
        for h in range(B_H):
            p.emit("dve", lambda E, h=h: E.tensor_scalar(
                out=o2[:, h * 128:(h + 1) * 128], in0=c.ps[:, h * 128:(h + 1) * 128],
                scalar1=rstd[:, h:h + 1], scalar2=None, op0=ALU.mult),
                reads=[og] + ps_tiles(c, h * 128, (h + 1) * 128) + streg, writes=o2reg)
        p.emit("dve", lambda E: E.tensor_mul(out=o2, in0=o2, in1=c.ac[:, 4096:6144]),
               reads=o2reg + mkreg, writes=o2reg)
        stg = c.st[:, (n % 2) * 2048:(n % 2) * 2048 + 2048]
        sb_ = c.st_b[n % 2]
        p.emit("dve", lambda E, gs=gs, stg=stg: E.tensor_mul(out=stg, in0=o2, in1=gs),
               reads=o2reg + [strm[5][r]], writes=[sb_])
        p.dma("sp", bo[n], stg, reads=[sb_], writes=[c.misc["d_bo"]])
        if STEPS < 9:
            continue
        mms = []
        for h in range(B_H):
            fn = (lambda E, h=h, kts=kts, es_=es_: E.matmul(
                c.ps[0:64, 2048 + h * 64:2048 + h * 64 + 64],
                kts[:, h * 64:(h + 1) * 64], es_[:, h * 64:(h + 1) * 64],
                start=True, stop=True))
            mms.append((fn, [strm[2][r], strm[3][r]], True, True))
        akg = mm_group(c, mms, ps_tiles(c, 2048, 3072), tag + f"ak{n}")
        mms = []
        for h in range(B_H):
            fn = (lambda E, h=h, es_=es_, vs=vs: E.matmul(
                c.ps[0:64, h * 128:(h + 1) * 128],
                es_[:, h * 64:(h + 1) * 64], vs[:, h * 128:(h + 1) * 128],
                start=True, stop=True))
            mms.append((fn, [strm[3][r], strm[4][r]], True, True))
        avg = mm_group(c, mms, ps_tiles(c, 0, 2048), tag + f"av{n}")
        p.emit("dve", lambda E: E.tensor_add(out=c.ac[0:64, 0:1024],
               in0=c.ac[0:64, 0:1024], in1=c.ps[0:64, 2048:3072]),
               reads=[akg] + ps_tiles(c, 2048, 3072) + sreg, writes=sreg)
        p.emit("dve", lambda E: E.tensor_add(out=c.ac[0:64, 1024:3072],
               in0=c.ac[0:64, 1024:3072], in1=c.ps[0:64, 0:2048]),
               reads=[avg] + ps_tiles(c, 0, 2048) + sreg, writes=sreg)
        p.emit("dve", lambda E: E.tensor_add(out=c.ac[0:1, 3072:4096],
               in0=c.ac[0:1, 3072:4096], in1=c.ps[0:1, 3072:4096]),
               reads=[csg] + ps_tiles(c, 3072, 4096) + sreg, writes=sreg)
        if n < B_NCH - 1:
            p.emit("dve", lambda E: E.tensor_copy(out=c.wb[0:64, 0:1024],
                   in_=c.ac[0:64, 0:1024]), reads=sreg, writes=wbst)
            p.emit("dve", lambda E: E.tensor_copy(out=c.wb[0:64, 1024:3072],
                   in_=c.ac[0:64, 1024:3072]), reads=sreg, writes=wbst)
            p.emit("dve", lambda E: E.tensor_copy(out=c.wb[0:1, 3072:4096],
                   in_=c.ac[0:1, 3072:4096]), reads=sreg, writes=wbst)


def emit_wo(c, Wo_ap, bo, xsrc_ap, xsrc_b, ydst_ap, ydst_b, tag="wo"):
    p = c.p
    psbf = c.ps[:].bitcast(BF16)
    for kc in range(KC):
        slot = c.wb[:, kc * 2048:kc * 2048 + 2048]
        p.dma("pool", slot, Wo_ap[kc * 128:(kc + 1) * 128, :], writes=[c.wb_b[kc]])
    ob = [Buf(tag + "o0"), Buf(tag + "o1")]
    fm = [Buf(tag + "f0"), Buf(tag + "f1")]
    xb = [Buf(tag + "x0"), Buf(tag + "x1")]
    for gi in range(4):
        r = gi % 2
        bos = c.xn[:, r * 8192:r * 8192 + 8192]
        p.dma("sp", bos.rearrange("p (x d) -> p x d", x=4), bo[gi * 4:gi * 4 + 4].rearrange("x p d -> p x d"),
              reads=[c.misc["d_bo"]], writes=[ob[r]] + xn_r(c, r * 8192, r * 8192 + 8192))
        ofm = c.xn[:, 16384 + r * 8192:16384 + r * 8192 + 8192]
        ofm_reg = xn_r(c, 16384 + r * 8192, 16384 + r * 8192 + 8192)
        trt = ps_tiles(c, 2048, 3072)
        for x in range(16):
            tg_ = Buf(tag + f"t{gi}_{x}")
            for cc in range(4):
                wr = list(trt) if cc == 0 else []
                if cc == 3:
                    wr.append(tg_)
                p.emit("pe", lambda E, x=x, cc=cc, bos=bos: E.transpose(
                    out=psbf[:, 4096 + cc * 128:4096 + (cc + 1) * 128],
                    in_=bos[:, cc * 2048 + x * 128:cc * 2048 + (x + 1) * 128],
                    identity=c.st[:, ST_IDENT:ST_IDENT + 128]),
                    reads=[ob[r], c.misc["ident"]], writes=wr)
            p.emit("dve", lambda E, x=x, ofm=ofm: E.tensor_copy(
                out=ofm[:, x * 512:(x + 1) * 512], in_=psbf[:, 4096:4608]),
                reads=[tg_] + trt, writes=ofm_reg)
        for oc2 in range(8):
            mms = []
            for kc in range(KC):
                for oc in range(2):
                    ti = oc
                    fn = (lambda E, kc=kc, oc=oc, oc2=oc2, ofm=ofm: E.matmul(
                        c.ps[:, oc * 512:(oc + 1) * 512],
                        c.wb[:, kc * 2048 + oc2 * 256 + oc * 128:kc * 2048 + oc2 * 256 + (oc + 1) * 128],
                        ofm[:, kc * 512:(kc + 1) * 512],
                        start=(kc == 0), stop=(kc == KC - 1)))
                    mms.append((fn, [c.wb_b[kc]] + ofm_reg, kc == 0, kc == KC - 1))
            grp = mm_group(c, mms, ps_tiles(c, 0, 1024), tag + f"g{gi}_{oc2}")
            xs = c.ac[:, r * 2048:r * 2048 + 1024]
            xreg = ac_r(c, r * 2048, r * 2048 + 1024)
            p.dma("sp", xs.rearrange("p (o d) -> p o d", o=2),
                  xsrc_ap[oc2 * 256:(oc2 + 1) * 256, gi * 512:(gi + 1) * 512].rearrange(
                      "(o p) d -> p o d", p=128),
                  reads=[xsrc_b], writes=xb[r] if False else xreg)
            for oc in range(2):
                p.emit("dve", lambda E, oc=oc, xs=xs: E.tensor_add(
                    out=xs[:, oc * 512:(oc + 1) * 512], in0=xs[:, oc * 512:(oc + 1) * 512],
                    in1=c.ps[:, oc * 512:(oc + 1) * 512]),
                    reads=[grp] + ps_tiles(c, oc * 512, (oc + 1) * 512) + xreg, writes=xreg)
            p.dma("sp", ydst_ap[oc2 * 256:(oc2 + 1) * 256, gi * 512:(gi + 1) * 512].rearrange(
                  "(o p) d -> p o d", p=128), xs.rearrange("p (o d) -> p o d", o=2),
                  reads=xreg, writes=[ydst_b])


def build_l3(stop_after="full"):
    nc = bass.Bass("TRN2", target_bir_lowering=False, debug=False)
    DT = lambda n, s, d, k: nc.dram_tensor(n, s, d, kind=k).ap()
    xres2 = DT("xres2", [D, T], F32, "ExternalInput")
    WqB = DT("WqB", [D, HID], F32, "ExternalInput")
    WkB = DT("WkB", [D, HID], F32, "ExternalInput")
    WsB = DT("WsB", [D, HID], F32, "ExternalInput")
    WvB = DT("WvB", [D, D], F32, "ExternalInput")
    WgB = DT("WgB", [D, D], F32, "ExternalInput")
    WoB = DT("WoB", [D, D], F32, "ExternalInput")
    akall = DT("akall", [8, 64, 1024], F32, "ExternalInput")
    avall = DT("avall", [8, 64, 2048], F32, "ExternalInput")
    srall = DT("srall", [8, 1, 1024], F32, "ExternalInput")
    gnw = DT("gnw", [128, 2048], F32, "ExternalInput")
    cbigB = DT("cbigB", [128, 2048], F32, "ExternalInput")
    gwT_r = DT("gwT_r", [128, 128], F32, "ExternalInput")
    w1T = DT("w1T", [E_EXP, D, HID], F32, "ExternalInput")
    w3T = DT("w3T", [E_EXP, D, HID], F32, "ExternalInput")
    w2T = DT("w2T", [E_EXP, HID, D], F32, "ExternalInput")
    sw1T = DT("sw1T", [D, HID], F32, "ExternalInput")
    sw3T = DT("sw3T", [D, HID], F32, "ExternalInput")
    sw2T = DT("sw2T", [HID, D], F32, "ExternalInput")
    csm = DT("csm", [128, 960], F32, "ExternalInput")
    cbf = DT("cbf", [128, 1280], BF16, "ExternalInput")
    yT = DT("yT", [D, T], F32, "ExternalOutput")
    bq = DT("bq", [8, 128, T], BF16, "Internal")
    bkfm = DT("bkfm", [8, 128, T], BF16, "Internal")
    bktm = DT("bktm", [16, 128, HID], BF16, "Internal")
    bE = DT("bE", [16, 128, HID], BF16, "Internal")
    bv = DT("bv", [16, 128, D], BF16, "Internal")
    bg = DT("bg", [16, 128, D], BF16, "Internal")
    bo = DT("bo", [16, 128, D], BF16, "Internal")

    es = ExitStack()
    with es:
        p = Prog(nc)
        c = make_ctx(nc, p, es)
        emit_init(c, csm, cbf)
        emit_ln(c, xres2, "l3a", xsrc_b=c.misc["d_x2"])
        emit_proj(c, WqB, HID, "fm", bq, scale=0.125, tag="q",
                  spill_buf=c.misc["d_bq"])
        emit_proj(c, WkB, HID, "fm", bkfm, tag="kf", spill_buf=c.misc["d_bkfm"])
        emit_proj(c, WkB, HID, "tm", bktm, tag="kt", spill_buf=c.misc["d_bktm"])
        emit_proj(c, WsB, HID, "tm", bE, act="expclip", tag="bs",
                  spill_buf=c.misc["d_bE"])
        emit_proj(c, WvB, D, "tm", bv, tag="bv", spill_buf=c.misc["d_bv"])
        emit_proj(c, WgB, D, "tm", bg, act="silu", tag="bg", spill_buf=c.misc["d_bg"])
        if stop_after != "proj":
            emit_combine_b(c, akall, avall, srall)
            emit_scan_b(c, bq, bkfm, bktm, bE, bv, bg, cbigB, gnw, bo)
        if stop_after in ("wo", "ln4", "full"):
            emit_wo(c, WoB, bo, xres2, c.misc["d_x2"], yT, c.misc["d_yT"])
        if stop_after in ("ln4", "full"):
            emit_ln(c, yT, "l3b", xsrc_b=c.misc["d_yT"])
        if stop_after == "full":
            emit_moe(c, gwT_r, w1T, w3T, w2T, sw1T, sw3T, sw2T, yT,
                     c.misc["d_yT"], tag="m2")
        p.finalize(es)
    return nc


def build_consts_small(core_id):
    gam = _gammas()
    sm = np.zeros((128, 960), np.float32)
    sm[:, SM_ONESF] = 1.0
    sm[0, SM_ONESR:SM_ONESR + 128] = 1.0
    for h in range(A_HEADS):
        for j in range(8):
            w = gam[h] ** (2048.0 * (core_id - 1 - j)) if j < core_id else 0.0
            sm[:, SM_AW + h * 8 + j] = w
    for j in range(8):
        sm[:, SM_AW01 + j] = 1.0 if j < core_id else 0.0
    for h in range(A_HEADS):
        for sc in range(2):
            pos = sc * 128 + np.arange(128)
            sm[:, SM_KVSC + h * 2 + sc] = gam[h] ** (255.0 - pos)
        for tt in range(16):
            pos = tt * 128 + np.arange(128)
            sm[:, SM_KVSF + h * 16 + tt] = gam[h] ** (2047.0 - pos)
    return sm


def build_cbf():
    cb = np.zeros((128, 1280), BF)
    cb[:, 0:128] = np.eye(128)
    for e in range(8):
        cb[e, 128 + e * 128:128 + (e + 1) * 128] = 1.0
    idx = np.arange(128)
    cb[:, 1152:1280] = (idx[:, None] <= idx[None, :]).astype(np.float32)
    return cb


def _f32c(a):
    return np.ascontiguousarray(np.asarray(a), dtype=np.float32)


def _pad_w17(W, b):
    Wp = np.zeros((2176, W.shape[1]), np.float32)
    Wp[:2048] = W
    Wp[2048] = b
    return Wp


def build_cbigA():
    gam = _gammas()
    idx = np.arange(256)
    cb = np.zeros((128, 6144), np.float32)
    for h in range(A_HEADS):
        m = np.where(idx[None, :] >= idx[:, None],
                     gam[h] ** (idx[None, :] - idx[:, None]), 0.0)
        for sc in range(2):
            cb[:, h * 512 + sc * 256:h * 512 + sc * 256 + 256] = m[sc * 128:(sc + 1) * 128, :]
        cb[:, 4096 + h * 256:4096 + h * 256 + 256] = gam[h] ** (idx + 1)[None, :]
    return cb


def build_cbigB():
    idx = np.arange(128)
    m = (idx[:, None] <= idx[None, :]).astype(np.float32)
    return np.ascontiguousarray(np.tile(m, (1, 16)))


_NC_CACHE = {}


def _progs():
    if "l1" not in _NC_CACHE:
        _NC_CACHE["l1"] = build_l1()
        _NC_CACHE["l2"] = build_l2()
        _NC_CACHE["l3"] = build_l3()
    return _NC_CACHE["l1"], _NC_CACHE["l2"], _NC_CACHE["l3"]


def kernel(**inputs):
    I = inputs
    x = np.asarray(I["L0_attn_a_x"])[0]
    f32 = np.float32
    nc1, nc2, nc3 = _progs()

    xTs = [np.ascontiguousarray(x[c * T:(c + 1) * T].T).astype(f32) for c in range(NCORE)]
    WkA = _pad_w17(np.asarray(I["L0_attn_a_Wk"]), np.asarray(I["L0_attn_a_bk"]))
    WvA = _pad_w17(np.asarray(I["L0_attn_a_Wv"]), np.asarray(I["L0_attn_a_bv"]))
    WqA = _f32c(I["L0_attn_a_Wq"])
    cbf = build_cbf()
    csms = []
    for c in range(NCORE):
        sm = build_consts_small(c)
        sm[:, SM_BIASQ:SM_BIASQ + 16] = np.asarray(I["L0_attn_a_bq"]).reshape(16, 128).T
        sm[:, SM_BIASK:SM_BIASK + 16] = np.asarray(I["L0_attn_a_bk"]).reshape(16, 128).T
        csms.append(sm)

    in1 = [{"xT": xTs[c], "Wk": WkA, "Wv": WvA, "csm": csms[c], "cbf": cbf}
           for c in range(NCORE)]
    res1 = run_bass_kernel_spmd(nc1, in1, core_ids=list(range(NCORE)))
    sall = np.stack([np.asarray(res1.results[c]["sloc"]) for c in range(NCORE)]).astype(f32)

    gw = np.asarray(I["L0_ffn_gate_w"])
    gwT_r = np.ascontiguousarray(
        gw.T.reshape(16, 128, 8).transpose(1, 0, 2).reshape(128, 128)).astype(f32)
    moe_common = {
        "gwT_r": gwT_r,
        "w1T": np.ascontiguousarray(np.asarray(I["L0_ffn_w1"]).transpose(0, 2, 1)).astype(f32),
        "w3T": np.ascontiguousarray(np.asarray(I["L0_ffn_w3"]).transpose(0, 2, 1)).astype(f32),
        "w2T": np.ascontiguousarray(np.asarray(I["L0_ffn_w2"]).transpose(0, 2, 1)).astype(f32),
        "sw1T": np.ascontiguousarray(np.asarray(I["L0_ffn_sw1"]).T).astype(f32),
        "sw3T": np.ascontiguousarray(np.asarray(I["L0_ffn_sw3"]).T).astype(f32),
        "sw2T": np.ascontiguousarray(np.asarray(I["L0_ffn_sw2"]).T).astype(f32),
    }
    cbigA = build_cbigA()
    in2 = [dict(moe_common, **{
        "xT": xTs[c], "Wq": WqA, "Wk": WkA, "Wv": WvA, "sall": sall,
        "cbigA": cbigA,
        "WkB": _f32c(I["L0_attn_b_Wk"]), "WsB": _f32c(I["L0_attn_b_Ws"]),
        "WvB": _f32c(I["L0_attn_b_Wv"]),
        "csm": csms[c], "cbf": cbf}) for c in range(NCORE)]
    res2 = run_bass_kernel_spmd(nc2, in2, core_ids=list(range(NCORE)))
    akall = np.stack([np.asarray(res2.results[c]["akb"]) for c in range(NCORE)]).astype(f32)
    avall = np.stack([np.asarray(res2.results[c]["avb"]) for c in range(NCORE)]).astype(f32)
    srall = np.stack([np.asarray(res2.results[c]["srb"]) for c in range(NCORE)]).astype(f32)

    gnw = np.asarray(I["L0_attn_b_g_norm_weight"])
    gnw_bc = np.ascontiguousarray(
        np.broadcast_to(np.tile(gnw, 16)[None, :], (128, 2048))).astype(f32)
    in3 = [dict(moe_common, **{
        "xres2": np.asarray(res2.results[c]["xres"]).astype(f32),
        "WqB": _f32c(I["L0_attn_b_Wq"]), "WkB": _f32c(I["L0_attn_b_Wk"]),
        "WsB": _f32c(I["L0_attn_b_Ws"]), "WvB": _f32c(I["L0_attn_b_Wv"]),
        "WgB": _f32c(I["L0_attn_b_Wg"]), "WoB": _f32c(I["L0_attn_b_Wo"]),
        "akall": akall, "avall": avall, "srall": srall,
        "gnw": gnw_bc, "cbigB": build_cbigB(),
        "csm": csms[c], "cbf": cbf}) for c in range(NCORE)]
    res3 = run_bass_kernel_spmd(nc3, in3, core_ids=list(range(NCORE)))

    out = np.empty((1, 16384, 2048), np.float32)
    for c in range(NCORE):
        out[0, c * T:(c + 1) * T, :] = np.asarray(res3.results[c]["yT"]).T
    return out


# version 07
# speedup vs baseline: 1.4669x; 1.4669x over previous
import math
from collections import defaultdict
from contextlib import ExitStack

import numpy as np
import ml_dtypes

import concourse.bass as bass
import concourse.mybir as mybir
from concourse.bass_utils import run_bass_kernel_spmd

F32 = mybir.dt.float32
BF16 = mybir.dt.bfloat16
AF = mybir.ActivationFunctionType
ALU = mybir.AluOpType
AX = mybir.AxisListType
BF = ml_dtypes.bfloat16

NCORE = 8
T = 2048
D = 2048
KC = D // 128

A_HEADS, A_DH, A_CHUNK = 8, 256, 256
A_NCH = T // A_CHUNK
B_H, B_DK, B_DV, B_M, B_CHUNK = 16, 64, 128, 64, 128
B_NCH = T // B_CHUNK
E_EXP, HID = 8, 1024
LN_EPS = 1e-5
B_EPS = 1e-5


def _gammas():
    betas = np.linspace(np.log2(1.0 / 32.0), np.log2(1.0 / 512.0), A_HEADS)
    return (1.0 - np.exp2(betas)).astype(np.float64)


class Buf:
    __slots__ = ("name", "writer", "readers")

    def __init__(self, name=""):
        self.name = name
        self.writer = None
        self.readers = []


class Prog:
    ENGINES = ("pe", "act", "dve", "pool", "sp")

    def __init__(self, nc):
        self.nc = nc
        self.ops = {e: [] for e in self.ENGINES}
        self.counts = defaultdict(int)

    def emit(self, eng, fn, reads=(), writes=(), inc=None, extra_waits=()):
        waits = list(extra_waits)
        for b in reads:
            if b.writer is not None:
                waits.append(b.writer)
        for b in writes:
            if b.writer is not None:
                waits.append(b.writer)
            waits.extend(b.readers)
        wmax = {}
        for (s, v) in waits:
            if wmax.get(s, -1) < v:
                wmax[s] = v
        sem = "s_" + eng
        if inc is None:
            inc = 16 if eng in ("pool", "sp") else 1
        self.counts[sem] += inc
        tok = (sem, self.counts[sem])
        self.ops[eng].append((fn, sorted(wmax.items()), inc))
        for b in reads:
            b.readers.append(tok)
        for b in writes:
            b.writer = tok
            b.readers = []
        return tok

    def dma(self, eng, out, in_, reads=(), writes=(), **kw):
        return self.emit(eng, lambda E: E.dma_start(out=out, in_=in_, **kw),
                         reads=reads, writes=writes)

    def fence(self):
        b = Buf("fence")
        ew = [(s, v) for s, v in self.counts.items()]
        self.emit("dve", lambda E: E.engine_nop(), extra_waits=ew, writes=[b])
        return b

    def finalize(self, es):
        nc = self.nc
        handles = {}
        for e in self.ENGINES:
            handles["s_" + e] = es.enter_context(nc.semaphore("s_" + e))
        blk = es.enter_context(nc.Block())
        ops = self.ops
        counts = dict(self.counts)

        def run(engine_obj, eng_name):
            waited = defaultdict(int)
            own = "s_" + eng_name
            is_dma = eng_name in ("pool", "sp")
            for fn, waits, inc in ops[eng_name]:
                for (s, v) in waits:
                    if s == own and not is_dma:
                        continue
                    if waited[s] >= v:
                        continue
                    engine_obj.wait_ge(handles[s], v)
                    waited[s] = v
                fn(engine_obj).then_inc(handles["s_" + eng_name], inc)

        @blk.tensor
        def _(e):
            run(e, "pe")

        @blk.scalar
        def _(e):
            run(e, "act")

        @blk.vector
        def _(e):
            run(e, "dve")

        @blk.gpsimd
        def _(e):
            run(e, "pool")
            for s in sorted(counts):
                e.wait_ge(handles[s], counts[s])

        @blk.sync
        def _(e):
            run(e, "sp")


class Ctx:
    pass


ST_RING = 0
ST_IDENT = 4224
ST_ONES_ROW = 4352
ST_ONES_COL = 4480
ST_BIAS = 6144

SM_ONESF = 0
SM_AW = 8
SM_AW01 = 72
SM_KVSC = 96
SM_KVSF = 128
SM_BIASQ = 256
SM_BIASK = 272
SM_SCRATCH = 288
SM_ONESR = 832


def make_ctx(nc, p, es):
    c = Ctx()
    c.nc, c.p = nc, p
    c.xn = es.enter_context(nc.sbuf_tensor("xn", [128, KC * T], BF16))
    c.wb = es.enter_context(nc.sbuf_tensor("wb", [128, 32768], BF16))
    c.ac = es.enter_context(nc.sbuf_tensor("ac", [128, 16384], F32))
    c.st = es.enter_context(nc.sbuf_tensor("st", [128, 6144], BF16))
    c.sm = es.enter_context(nc.sbuf_tensor("sm", [128, 960], F32))
    c.ps = es.enter_context(nc.psum_tensor("ps", [128, 4096], F32))
    c.psb = [Buf(f"ps{i}") for i in range(8)]
    c.xn_b = [Buf(f"xn{i}") for i in range(KC)]
    c.wb_b = [Buf(f"wb{i}") for i in range(16)]
    c.ac_b = [Buf(f"ac{i}") for i in range(8)]
    c.st_b = [Buf(f"st{i}") for i in range(3)]
    c.misc = defaultdict(Buf)
    return c


def ac_r(c, col0, col1):
    return [c.ac_b[i] for i in range(col0 // 2048, (col1 + 2047) // 2048)]


def wb_r(c, col0, col1):
    return [c.wb_b[i] for i in range(col0 // 2048, (col1 + 2047) // 2048)]


def st_r(c, col0, col1):
    return [c.st_b[i] for i in range(col0 // 2048, (col1 + 2047) // 2048)]


def xn_r(c, col0, col1):
    return [c.xn_b[i] for i in range(col0 // 2048, (col1 + 2047) // 2048)]


def bias_row_ap(c, n, slot=0):
    return c.ac[:].bitcast(BF16)[0:1, 28672 + slot * 2048:28672 + slot * 2048 + n]


def ps_tiles(c, col0, col1):
    return [c.psb[i] for i in range(col0 // 512, (col1 + 511) // 512)]


def mm_group(c, mms, tiles, tag=""):
    p = c.p
    grp = Buf(tag)
    n = len(mms)
    for i, (fn, rd, st_, sp_) in enumerate(mms):
        wr = []
        if i == 0:
            wr += tiles
        if i == n - 1:
            wr.append(grp)
        p.emit("pe", fn, reads=rd, writes=wr)
    return grp


def emit_init(c, csm_ap, cbf_ap):
    p = c.p
    p.dma("sp", c.sm[:], csm_ap, writes=[c.misc["sm"]])
    p.dma("sp", c.st[:, ST_IDENT:ST_IDENT + 128], cbf_ap[:, 0:128], writes=[c.misc["ident"]])
    p.dma("sp", c.st[:, 4736:5760], cbf_ap[:, 128:1152],
          writes=[c.misc["esel"]])
    p.dma("sp", c.st[:, 5760:5888], cbf_ap[:, 1152:1280],
          writes=[c.misc["lt"]])
    p.emit("pool", lambda E: E.memset(c.st[0:1, ST_ONES_ROW:ST_ONES_ROW + 128], 1.0),
           writes=[c.misc["ones_row"]], inc=1)
    p.emit("pool", lambda E: E.memset(c.st[:, ST_ONES_COL:ST_ONES_COL + 1], 1.0),
           writes=[c.misc["ones_col"]], inc=1)
    p.emit("pool", lambda E: E.memset(c.st[:, 4608:4736], 0.0),
           writes=[c.misc["ones_p0"]], inc=1)
    p.emit("pool", lambda E: E.memset(c.st[0:1, 4608:4736], 1.0),
           writes=[c.misc["ones_p0"]], inc=1)



def emit_ln(c, xsrc_ap, tag="", xsrc_b=None):
    p = c.p
    onesf = c.sm[:, SM_ONESF:SM_ONESF + 1]
    smb = c.misc["sm"]
    xreg = [c.ac_b[0], c.ac_b[1]]
    sqreg = [c.ac_b[2]]
    stats_tiles = ps_tiles(c, 0, 4096)
    grp = Buf(tag + "lgrp")
    for kc in range(KC):
        ring = kc % 2
        xs = c.ac[:, ring * 2048:ring * 2048 + 2048]
        p.dma("sp", xs, xsrc_ap[kc * 128:(kc + 1) * 128, :],
              reads=([xsrc_b] if xsrc_b else []), writes=[xreg[ring]])
        sq = c.ac[:, 4096:6144]
        p.emit("act", lambda E, xs=xs, sq=sq: E.activation(out=sq, in_=xs, func=AF.Square),
               reads=[xreg[ring]], writes=sqreg)
        for tg in range(4):
            p.emit("pe", lambda E, xs=xs, tg=tg, kc=kc: E.matmul(
                c.ps[0:1, tg * 512:(tg + 1) * 512], onesf,
                xs[:, tg * 512:(tg + 1) * 512], start=(kc == 0), stop=(kc == KC - 1)),
                reads=[xreg[ring], smb],
                writes=(stats_tiles if (kc == 0 and tg == 0) else []))
            p.emit("pe", lambda E, sq=sq, tg=tg, kc=kc: E.matmul(
                c.ps[0:1, 2048 + tg * 512:2048 + (tg + 1) * 512], onesf,
                sq[:, tg * 512:(tg + 1) * 512], start=(kc == 0), stop=(kc == KC - 1)),
                reads=sqreg + [smb],
                writes=([grp] if (kc == KC - 1 and tg == 3) else []))
    m_row = c.ac[0:1, 10240:12288]
    v_row = c.ac[0:1, 12288:14336]
    mr_row = c.ac[0:1, 14336:16384]
    mreg, vreg, mrreg = [c.ac_b[5]], [c.ac_b[6]], [c.ac_b[7]]
    p.emit("dve", lambda E: E.tensor_scalar(out=m_row, in0=c.ps[0:1, 0:2048],
           scalar1=1.0 / D, scalar2=None, op0=ALU.mult),
           reads=[grp] + stats_tiles, writes=mreg)
    p.emit("dve", lambda E: E.tensor_scalar(out=v_row, in0=c.ps[0:1, 2048:4096],
           scalar1=1.0 / D, scalar2=None, op0=ALU.mult),
           reads=[grp] + stats_tiles, writes=vreg)
    p.emit("dve", lambda E: E.tensor_mul(out=mr_row, in0=m_row, in1=m_row),
           reads=mreg, writes=mrreg)
    p.emit("dve", lambda E: E.tensor_sub(out=v_row, in0=v_row, in1=mr_row),
           reads=vreg + mrreg, writes=vreg)
    p.emit("dve", lambda E: E.tensor_scalar(out=v_row, in0=v_row, scalar1=LN_EPS,
           scalar2=None, op0=ALU.add), reads=vreg, writes=vreg)
    p.emit("act", lambda E: E.activation(out=v_row, in_=v_row, func=AF.Ln),
           reads=vreg, writes=vreg)
    p.emit("act", lambda E: E.activation(out=v_row, in_=v_row, func=AF.Exp, scale=-0.5),
           reads=vreg, writes=vreg)
    p.emit("dve", lambda E: E.tensor_mul(out=mr_row, in0=m_row, in1=v_row),
           reads=mreg + vreg, writes=mrreg)
    rbc = c.ac[:, 6144:8192]
    mrbc = c.ac[:, 8192:10240]
    rbcreg, mrbcreg = [c.ac_b[3]], [c.ac_b[4]]
    bt = ps_tiles(c, 0, 4096)
    bgrp = Buf(tag + "lbg")
    onesr = c.sm[0:1, SM_ONESR:SM_ONESR + 128]
    for tg in range(4):
        p.emit("pe", lambda E, tg=tg: E.matmul(
            c.ps[:, tg * 512:(tg + 1) * 512], onesr, v_row[:, tg * 512:(tg + 1) * 512],
            start=True, stop=True), reads=vreg + [smb],
            writes=(bt if tg == 0 else []))
        p.emit("pe", lambda E, tg=tg: E.matmul(
            c.ps[:, 2048 + tg * 512:2048 + (tg + 1) * 512], onesr,
            mr_row[:, tg * 512:(tg + 1) * 512], start=True, stop=True),
            reads=mrreg + [smb], writes=([bgrp] if tg == 3 else []))
    p.emit("dve", lambda E: E.tensor_copy(out=rbc, in_=c.ps[:, 0:2048]),
           reads=[bgrp] + bt, writes=rbcreg)
    p.emit("dve", lambda E: E.tensor_copy(out=mrbc, in_=c.ps[:, 2048:4096]),
           reads=[bgrp] + bt, writes=mrbcreg)
    for kc in range(KC):
        ring = kc % 2
        xs = c.ac[:, ring * 2048:ring * 2048 + 2048]
        p.dma("sp", xs, xsrc_ap[kc * 128:(kc + 1) * 128, :],
              reads=([xsrc_b] if xsrc_b else []), writes=[xreg[ring]])
        sq = c.ac[:, 4096:6144]
        p.emit("dve", lambda E, xs=xs, sq=sq: E.tensor_mul(out=sq, in0=xs, in1=rbc),
               reads=[xreg[ring]] + rbcreg, writes=sqreg)
        p.emit("dve", lambda E, sq=sq, kc=kc: E.tensor_sub(
            out=c.xn[:, kc * T:kc * T + 2048], in0=sq, in1=mrbc),
            reads=sqreg + mrbcreg, writes=[c.xn_b[kc]])


PROJ_SLOT = [0]


def emit_proj(c, W_ap, n_out, mode, spill_ap, bias=None, act=None, scale=None, tag="", spill_buf=None):
    p = c.p
    smb = c.misc["sm"]
    bslot = PROJ_SLOT[0] % 2
    PROJ_SLOT[0] += 1
    for kc in range(KC):
        slot = c.wb[:, kc * 2048:kc * 2048 + n_out]
        p.dma("pool", slot, W_ap[kc * 128:(kc + 1) * 128, :], writes=[c.wb_b[kc]])
    brow_b = None
    if bias is not None and bias[0] == "w17":
        brow_b = c.ac_b[7]
        w17 = c.ac[:].bitcast(BF16)[:, 28672:28672 + n_out]
        p.dma("pool", w17, W_ap[2048:2176, :], writes=[brow_b])

    if mode == "fm":
        OC2 = n_out // 256
        for oc2 in range(OC2):
            grps = []
            for oc in range(2):
                par = (oc2 * 2 + oc) % 2
                bt = par * 4
                mms = []
                for kc in range(KC):
                    for tg in range(4):
                        ti = bt + tg
                        fn = (lambda E, kc=kc, oc=oc, tg=tg, ti=ti, oc2=oc2: E.matmul(
                            c.ps[:, ti * 512:(ti + 1) * 512],
                            c.wb[:, kc * 2048 + oc2 * 256 + oc * 128:
                                    kc * 2048 + oc2 * 256 + (oc + 1) * 128],
                            c.xn[:, kc * T + tg * 512:kc * T + (tg + 1) * 512],
                            start=(kc == 0), stop=(kc == KC - 1)))
                        mms.append((fn, [c.wb_b[kc], c.xn_b[kc]], kc == 0, kc == KC - 1))
                grps.append(mm_group(c, mms, ps_tiles(c, bt * 512, (bt + 4) * 512),
                                     tag + f"g{oc2}_{oc}"))
            for oc in range(2):
                par = (oc2 * 2 + oc) % 2
                bt = par * 4
                grp = grps[oc]
                slot = (oc2 * 2 + oc) % 2
                stg = c.st[:, slot * 2048:slot * 2048 + 2048]
                sreg = [c.st_b[slot]]
                for tg in range(4):
                    ti = bt + tg
                    rd = [grp] + ps_tiles(c, ti * 512, (ti + 1) * 512) + [smb]
                    so_ = tg * 512
                    if bias is not None:
                        bcol = c.sm[:, bias[1] + oc2 * 2 + oc:bias[1] + oc2 * 2 + oc + 1]
                        p.emit("act", lambda E, ti=ti, so_=so_, bcol=bcol, stg=stg: E.activation(
                            out=stg[:, so_:so_ + 512],
                            in_=c.ps[:, ti * 512:(ti + 1) * 512],
                            func=AF.Identity, bias=bcol), reads=rd, writes=sreg)
                    elif scale is not None:
                        p.emit("act", lambda E, ti=ti, so_=so_, stg=stg: E.activation(
                            out=stg[:, so_:so_ + 512],
                            in_=c.ps[:, ti * 512:(ti + 1) * 512],
                            func=AF.Copy, scale=scale), reads=rd, writes=sreg)
                    else:
                        p.emit("act", lambda E, ti=ti, so_=so_, stg=stg: E.activation(
                            out=stg[:, so_:so_ + 512],
                            in_=c.ps[:, ti * 512:(ti + 1) * 512],
                            func=AF.Copy), reads=rd, writes=sreg)
                p.dma("sp", spill_ap[oc2 * 2 + oc], stg, reads=sreg,
                      writes=[spill_buf or Buf("spl")])
    else:
        NG = n_out // 512
        for tt in range(16):
            par = tt % 2
            base = par * 4
            mms = []
            for kc in range(KC):
                for og in range(NG):
                    ti = base + og
                    fn = (lambda E, kc=kc, og=og, ti=ti, tt=tt: E.matmul(
                        c.ps[:, ti * 512:(ti + 1) * 512],
                        c.xn[:, kc * T + tt * 128:kc * T + (tt + 1) * 128],
                        c.wb[:, kc * 2048 + og * 512:kc * 2048 + (og + 1) * 512],
                        start=(kc == 0),
                        stop=(kc == KC - 1 and bias is None)))
                    mms.append((fn, [c.wb_b[kc], c.xn_b[kc]], kc == 0, False))
            if bias is not None:
                w17 = c.ac[:].bitcast(BF16)[:, 28672:28672 + n_out]
                for og in range(NG):
                    ti = base + og
                    fn = (lambda E, og=og, ti=ti, w17=w17: E.matmul(
                        c.ps[:, ti * 512:(ti + 1) * 512],
                        c.st[:, 4608:4736],
                        w17[:, og * 512:(og + 1) * 512],
                        start=False, stop=True))
                    mms.append((fn, [brow_b, c.misc["ones_p0"]], False, True))
            grp = mm_group(c, mms, ps_tiles(c, base * 512, (base + NG) * 512), tag + f"t{tt}")
            stg = c.st[:, par * 2048:par * 2048 + 2048]
            sreg = st_r(c, par * 2048, par * 2048 + 2048)
            for og in range(NG):
                ti = base + og
                src_ = c.ps[:, ti * 512:(ti + 1) * 512]
                dst = stg[:, og * 512:(og + 1) * 512]
                rd = [grp] + ps_tiles(c, ti * 512, (ti + 1) * 512)
                if act == "silu":
                    p.emit("act", lambda E, src_=src_, dst=dst: E.activation(
                        out=dst, in_=src_, func=AF.Silu), reads=rd, writes=sreg)
                elif act == "expclip":
                    p.emit("dve", lambda E, src_=src_: E.tensor_scalar(
                        out=src_, in0=src_, scalar1=-32.0, scalar2=32.0,
                        op0=ALU.max, op1=ALU.min), reads=rd,
                        writes=ps_tiles(c, ti * 512, (ti + 1) * 512))
                    p.emit("act", lambda E, src_=src_, dst=dst: E.activation(
                        out=dst, in_=src_, func=AF.Exp),
                        reads=[grp] + ps_tiles(c, ti * 512, (ti + 1) * 512), writes=sreg)
                elif scale is not None:
                    p.emit("act", lambda E, src_=src_, dst=dst: E.activation(
                        out=dst, in_=src_, func=AF.Copy, scale=scale),
                        reads=rd, writes=sreg)
                else:
                    p.emit("act", lambda E, src_=src_, dst=dst: E.activation(
                        out=dst, in_=src_, func=AF.Copy), reads=rd, writes=sreg)
            p.dma("sp", spill_ap[tt], stg[:, 0:n_out], reads=sreg,
                  writes=[spill_buf or Buf("spl")])


def emit_summary_a(c, ktm_ap, vtm_ap, sloc_ap):
    p = c.p
    smb = c.misc["sm"]
    kb = [c.xn_b[0], c.xn_b[1]]
    vb = [c.xn_b[2], c.xn_b[3]]
    scb = c.wb_b[0]
    tiles = c.psb
    grp = Buf("sagrp")
    for tt in range(16):
        ring = tt % 2
        kslot = c.xn[:, ring * 2048:(ring + 1) * 2048]
        vslot = c.xn[:, 4096 + ring * 2048:4096 + (ring + 1) * 2048]
        p.dma("sp", kslot, ktm_ap[tt], writes=[kb[ring]])
        p.dma("sp", vslot, vtm_ap[tt], writes=[vb[ring]])
        ksc = c.wb[:, 0:2048]
        for h in range(A_HEADS):
            p.emit("dve", lambda E, h=h, kslot=kslot, tt=tt: E.tensor_scalar(
                out=ksc[:, h * 256:(h + 1) * 256],
                in0=kslot[:, h * 256:(h + 1) * 256],
                scalar1=c.sm[:, SM_KVSF + h * 16 + tt:SM_KVSF + h * 16 + tt + 1],
                scalar2=None, op0=ALU.mult),
                reads=[kb[ring], smb], writes=[scb])
        for h in range(A_HEADS):
            for dkc in range(2):
                ti = h * 2 + dkc
                wr = []
                if tt == 0 and ti == 0:
                    wr += tiles
                if tt == 15 and ti == 15:
                    wr.append(grp)
                p.emit("pe", lambda E, h=h, dkc=dkc, ti=ti, ksc=ksc, vslot=vslot, tt=tt: E.matmul(
                    c.ps[:, ti * 256:(ti + 1) * 256],
                    ksc[:, h * 256 + dkc * 128:h * 256 + (dkc + 1) * 128],
                    vslot[:, h * 256:(h + 1) * 256],
                    start=(tt == 0), stop=(tt == 15)),
                    reads=[scb, vb[ring]], writes=wr)
    eb = Buf("saev")
    p.emit("dve", lambda E: E.tensor_copy(out=c.ac[:, 0:4096], in_=c.ps[:, 0:4096]),
           reads=[grp] + tiles, writes=ac_r(c, 0, 4096) + [eb])
    p.dma("sp", sloc_ap.rearrange("(x p) d -> p x d", p=128),
          c.ac[:, 0:4096].rearrange("p (x d) -> p x d", x=16),
          reads=[eb], writes=[Buf("out")])


def build_l1():
    nc = bass.Bass("TRN2", target_bir_lowering=False, debug=False)
    xT = nc.dram_tensor("xT", [D, T], F32, kind="ExternalInput").ap()
    Wk = nc.dram_tensor("Wk", [2176, D], F32, kind="ExternalInput").ap()
    Wv = nc.dram_tensor("Wv", [2176, D], F32, kind="ExternalInput").ap()
    csm = nc.dram_tensor("csm", [128, 960], F32, kind="ExternalInput").ap()
    cbf = nc.dram_tensor("cbf", [128, 1280], BF16, kind="ExternalInput").ap()
    sloc = nc.dram_tensor("sloc", [2048, 256], F32, kind="ExternalOutput").ap()
    xn1 = nc.dram_tensor("xn1", [16, 128, T], BF16, kind="ExternalOutput").ap()
    ktm = nc.dram_tensor("aktm", [16, 128, D], BF16, kind="ExternalOutput").ap()
    vtm = nc.dram_tensor("av", [16, 128, D], BF16, kind="ExternalOutput").ap()
    es = ExitStack()
    with es:
        p = Prog(nc)
        c = make_ctx(nc, p, es)
        emit_init(c, csm, cbf)
        emit_ln(c, xT, "l1")
        for kc in range(KC):
            p.dma("sp", xn1[kc], c.xn[:, kc * T:(kc + 1) * T],
                  reads=[c.xn_b[kc]], writes=[c.misc["d_xn1"]])
        emit_proj(c, Wk, D, "tm", ktm, bias=("w17",), tag="k")
        emit_proj(c, Wv, D, "tm", vtm, bias=("w17",), tag="v")
        emit_summary_a(c, ktm, vtm, sloc)
        p.finalize(es)
    return nc


def emit_combine_a(c, sall_ap):
    p = c.p
    smb = c.misc["sm"]
    S = c.ac[:, 4096:8192]
    sreg = ac_r(c, 4096, 8192)
    p.emit("pool", lambda E: E.memset(S, 0.0), writes=sreg, inc=1)
    for j in range(8):
        off = 0 if j % 2 == 0 else 8192
        stage = c.ac[:, off:off + 4096]
        streg = ac_r(c, off, off + 4096)
        p.dma("sp", stage.rearrange("p (x d) -> p x d", x=16),
              sall_ap[j].rearrange("(x p) d -> p x d", p=128), writes=streg)
        for h in range(A_HEADS):
            p.emit("dve", lambda E, h=h, stage=stage, j=j: E.scalar_tensor_tensor(
                out=S[:, h * 512:(h + 1) * 512], in0=stage[:, h * 512:(h + 1) * 512],
                scalar=c.sm[:, SM_AW + h * 8 + j:SM_AW + h * 8 + j + 1],
                in1=S[:, h * 512:(h + 1) * 512], op0=ALU.mult, op1=ALU.add),
                reads=streg + [smb] + sreg, writes=sreg)


def emit_scan_a(c, aq, akfm, aktm, av, cbigA_ap, xsrc_ap, xdst_ap, xsrc_b, xdst_b):
    p = c.p
    smb = c.misc["sm"]
    gam = _gammas()
    acbf = c.ac[:].bitcast(BF16)
    mk_reg = ac_r(c, 8192, 14336)
    p.dma("sp", c.ac[:, 8192:14336], cbigA_ap, writes=mk_reg)
    S = c.ac[:, 4096:8192]
    sreg = ac_r(c, 4096, 8192)
    Sbf = c.wb[:, 0:4096]
    sbf_reg = wb_r(c, 0, 4096)
    scr_reg = wb_r(c, 4096, 6144)
    KQm = c.wb[:, 4096:4608]
    qsc = c.wb[:, 4608:5120]
    ksc = c.wb[:, 5120:5632]
    p.emit("dve", lambda E: E.tensor_copy(out=Sbf, in_=S), reads=sreg, writes=sbf_reg)
    qb = [Buf("scq0"), Buf("scq1")]
    kfb = [Buf("sckf0"), Buf("sckf1")]
    ktb = [Buf("sckt0"), Buf("sckt1")]
    vb = [Buf("scv0"), Buf("scv1")]
    for n in range(A_NCH):
        r = n % 2
        xacc = c.ac[:, 0:4096]
        xreg = ac_r(c, 0, 4096)
        qs = c.xn[:, r * 4096:r * 4096 + 4096]
        kfs = c.xn[:, 8192 + r * 4096:8192 + r * 4096 + 4096]
        kts = c.xn[:, 16384 + r * 4096:16384 + r * 4096 + 4096]
        vs = c.xn[:, 24576 + r * 4096:24576 + r * 4096 + 4096]
        p.dma("sp", qs.rearrange("p (x d) -> p x d", x=16),
              aq.rearrange("x p d -> p x d")[:, :, n * 256:(n + 1) * 256],
              reads=[c.misc["d_aq"]],
              writes=[qb[r]] + xn_r(c, r * 4096, r * 4096 + 4096))
        p.dma("sp", kfs.rearrange("p (x d) -> p x d", x=16),
              akfm.rearrange("x p d -> p x d")[:, :, n * 256:(n + 1) * 256],
              reads=[c.misc["d_akfm"]],
              writes=[kfb[r]] + xn_r(c, 8192 + r * 4096, 8192 + r * 4096 + 4096))
        p.dma("sp", kts.rearrange("p (x d) -> p x d", x=2),
              aktm[2 * n:2 * n + 2].rearrange("x p d -> p x d"),
              reads=[c.misc["d_aktm"]],
              writes=[ktb[r]] + xn_r(c, 16384 + r * 4096, 16384 + r * 4096 + 4096))
        p.dma("sp", vs.rearrange("p (x d) -> p x d", x=2),
              av[2 * n:2 * n + 2].rearrange("x p d -> p x d"),
              reads=[c.misc["d_av"]],
              writes=[vb[r]] + xn_r(c, 24576 + r * 4096, 24576 + r * 4096 + 4096))
        p.dma("sp", xacc.rearrange("p (x d) -> p x d", x=16),
              xsrc_ap.rearrange("(x p) t -> p x t", p=128)[:, :, n * 256:(n + 1) * 256],
              reads=[xsrc_b], writes=xreg)
        for h in range(A_HEADS):
            pb = (h % 2) * 2048
            kq_t = ps_tiles(c, pb, pb + 512)
            out_t = ps_tiles(c, pb + 512, pb + 1024)
            st_t = ps_tiles(c, pb + 1024, pb + 1536)
            mms = []
            for sc in range(2):
                for dkc in range(2):
                    x = h * 2 + dkc
                    fn = (lambda E, sc=sc, x=x, dkc=dkc, kfs=kfs, qs=qs, pb=pb: E.matmul(
                        c.ps[:, pb + sc * 256:pb + sc * 256 + 256],
                        kfs[:, x * 256 + sc * 128:x * 256 + sc * 128 + 128],
                        qs[:, x * 256:x * 256 + 256],
                        start=(dkc == 0), stop=(dkc == 1)))
                    mms.append((fn, [kfb[r], qb[r]], dkc == 0, dkc == 1))
            kqg = mm_group(c, mms, kq_t, f"kq{n}_{h}")
            for sc in range(2):
                p.emit("dve", lambda E, sc=sc, h=h, pb=pb: E.tensor_mul(
                    out=KQm[:, sc * 256:sc * 256 + 256],
                    in0=c.ps[:, pb + sc * 256:pb + sc * 256 + 256],
                    in1=c.ac[:, 8192 + h * 512 + sc * 256:8192 + h * 512 + sc * 256 + 256]),
                    reads=[kqg] + kq_t + mk_reg, writes=scr_reg)
            for dkc in range(2):
                x = h * 2 + dkc
                p.emit("dve", lambda E, x=x, dkc=dkc, h=h, qs=qs: E.tensor_mul(
                    out=qsc[:, dkc * 256:dkc * 256 + 256],
                    in0=qs[:, x * 256:x * 256 + 256],
                    in1=c.ac[:, 12288 + h * 256:12288 + h * 256 + 256]),
                    reads=[qb[r]] + mk_reg, writes=scr_reg)
            mms = []
            for dvc in range(2):
                for tt_s in range(2):
                    fn = (lambda E, dvc=dvc, tt_s=tt_s, h=h, vs=vs, pb=pb: E.matmul(
                        c.ps[:, pb + 512 + dvc * 256:pb + 512 + dvc * 256 + 256],
                        vs[:, tt_s * 2048 + h * 256 + dvc * 128:tt_s * 2048 + h * 256 + dvc * 128 + 128],
                        KQm[:, tt_s * 256:tt_s * 256 + 256],
                        start=(tt_s == 0), stop=False))
                    mms.append((fn, [vb[r]] + scr_reg, tt_s == 0, False))
                for dkc in range(2):
                    x = h * 2 + dkc
                    fn = (lambda E, dvc=dvc, dkc=dkc, x=x, pb=pb: E.matmul(
                        c.ps[:, pb + 512 + dvc * 256:pb + 512 + dvc * 256 + 256],
                        Sbf[:, x * 256 + dvc * 128:x * 256 + dvc * 128 + 128],
                        qsc[:, dkc * 256:dkc * 256 + 256],
                        start=False, stop=(dkc == 1)))
                    mms.append((fn, sbf_reg + scr_reg, False, dkc == 1))
            outg = mm_group(c, mms, out_t, f"oc{n}_{h}")
            for dvc in range(2):
                x = h * 2 + dvc
                p.emit("dve", lambda E, x=x, dvc=dvc, pb=pb: E.tensor_add(
                    out=xacc[:, x * 256:x * 256 + 256],
                    in0=xacc[:, x * 256:x * 256 + 256],
                    in1=c.ps[:, pb + 512 + dvc * 256:pb + 512 + dvc * 256 + 256]),
                    reads=[outg] + out_t + xreg, writes=xreg)
            for tt_s in range(2):
                p.emit("dve", lambda E, tt_s=tt_s, h=h, kts=kts: E.tensor_scalar(
                    out=ksc[:, tt_s * 256:tt_s * 256 + 256],
                    in0=kts[:, tt_s * 2048 + h * 256:tt_s * 2048 + h * 256 + 256],
                    scalar1=c.sm[:, SM_KVSC + h * 2 + tt_s:SM_KVSC + h * 2 + tt_s + 1],
                    scalar2=None, op0=ALU.mult),
                    reads=[ktb[r], smb], writes=scr_reg)
            mms = []
            for dkc in range(2):
                for tt_s in range(2):
                    fn = (lambda E, dkc=dkc, tt_s=tt_s, h=h, vs=vs, pb=pb: E.matmul(
                        c.ps[:, pb + 1024 + dkc * 256:pb + 1024 + dkc * 256 + 256],
                        ksc[:, tt_s * 256 + dkc * 128:tt_s * 256 + dkc * 128 + 128],
                        vs[:, tt_s * 2048 + h * 256:tt_s * 2048 + h * 256 + 256],
                        start=(tt_s == 0), stop=(tt_s == 1)))
                    mms.append((fn, [vb[r]] + scr_reg, tt_s == 0, tt_s == 1))
            stg_ = mm_group(c, mms, st_t, f"sd{n}_{h}")
            for dkc in range(2):
                x = h * 2 + dkc
                p.emit("dve", lambda E, x=x, dkc=dkc, h=h, pb=pb: E.scalar_tensor_tensor(
                    out=S[:, x * 256:x * 256 + 256], in0=S[:, x * 256:x * 256 + 256],
                    scalar=float(gam[h] ** A_CHUNK),
                    in1=c.ps[:, pb + 1024 + dkc * 256:pb + 1024 + dkc * 256 + 256],
                    op0=ALU.mult, op1=ALU.add),
                    reads=[stg_] + st_t + sreg, writes=sreg)
        p.dma("sp", xdst_ap.rearrange("(x p) t -> p x t", p=128)[:, :, n * 256:(n + 1) * 256],
              xacc.rearrange("p (x d) -> p x d", x=16), reads=xreg, writes=[xdst_b])
        if n < A_NCH - 1:
            p.emit("dve", lambda E: E.tensor_copy(out=Sbf, in_=S), reads=sreg,
                   writes=sbf_reg)


MO_GW = 0
MO_SBF = 128
MO_ROW = 256
MO_S8 = 2048
MO_BC = 4096
MO_U1 = 6144
MO_U3 = 14336
MO_WS = 22528
SC_SCORES = 288
SC_SIG = 416
SC_M1 = 544
SC_M2 = 560
SC_T8 = 576


def emit_moe_cache(c, w1T_ap, w3T_ap, w2T_ap, sw1T_ap, sw3T_ap, sw2T_ap):
    p = c.p
    nc = c.nc
    c.w1c = nc.dram_tensor("w1c", [E_EXP, D, HID], BF16, kind="Internal").ap()
    c.w3c = nc.dram_tensor("w3c", [E_EXP, D, HID], BF16, kind="Internal").ap()
    c.w2c = nc.dram_tensor("w2c", [E_EXP, HID, D], BF16, kind="Internal").ap()
    c.sw1c = nc.dram_tensor("sw1c", [D, HID], BF16, kind="Internal").ap()
    c.sw3c = nc.dram_tensor("sw3c", [D, HID], BF16, kind="Internal").ap()
    c.sw2c = nc.dram_tensor("sw2c", [HID, D], BF16, kind="Internal").ap()
    cb = c.misc["d_moec"]
    for e in range(E_EXP):
        p.dma("pool", c.w1c[e], w1T_ap[e], writes=[cb])
        p.dma("pool", c.w3c[e], w3T_ap[e], writes=[cb])
        p.dma("pool", c.w2c[e], w2T_ap[e], writes=[cb])
    p.dma("pool", c.sw1c, sw1T_ap, writes=[cb])
    p.dma("pool", c.sw3c, sw3T_ap, writes=[cb])
    p.dma("pool", c.sw2c, sw2T_ap, writes=[cb])


def emit_moe(c, gwT_r_ap, w1T_ap, w3T_ap, w2T_ap, sw1T_ap, sw3T_ap, sw2T_ap,
             xdst_ap, xdst_b, tag=""):
    p = c.p
    smb = c.misc["sm"]
    gwb = wb_r(c, MO_GW, MO_GW + 128)
    p.dma("pool", c.wb[:, MO_GW:MO_GW + 128], gwT_r_ap, writes=gwb)
    gl_t = ps_tiles(c, 3584, 4096)
    mms = []
    for tt in range(16):
        for kc in range(KC):
            fn = (lambda E, tt=tt, kc=kc: E.matmul(
                c.ps[:, 3584 + tt * 8:3584 + tt * 8 + 8],
                c.xn[:, kc * T + tt * 128:kc * T + (tt + 1) * 128],
                c.wb[:, MO_GW + kc * 8:MO_GW + kc * 8 + 8],
                start=(kc == 0), stop=(kc == KC - 1)))
            mms.append((fn, [c.xn_b[kc]] + gwb, kc == 0, kc == KC - 1))
    glg = mm_group(c, mms, gl_t, tag + "gl")
    scores = c.sm[:, SC_SCORES:SC_SCORES + 128]
    sig = c.sm[:, SC_SIG:SC_SIG + 128]
    scb = Buf(tag + "scores")
    sigb = Buf(tag + "sig")
    p.emit("act", lambda E: E.activation(out=scores, in_=c.ps[:, 3584:3712],
           func=AF.Sigmoid), reads=[glg] + gl_t + [smb], writes=[scb])
    m1 = c.sm[:, SC_M1:SC_M1 + 16]
    m2 = c.sm[:, SC_M2:SC_M2 + 16]
    t8 = c.sm[:, SC_T8:SC_T8 + 128]
    m1b, m2b, t8b = Buf(tag + "m1"), Buf(tag + "m2"), Buf(tag + "t8")
    p.emit("dve", lambda E: E.reduce_max(out=m1, in_=scores.rearrange(
        "p (a b) -> p a b", a=16), axis=AX.X), reads=[scb], writes=[m1b])
    for tt in range(16):
        p.emit("dve", lambda E, tt=tt: E.tensor_scalar(
            out=t8[:, tt * 8:tt * 8 + 8], in0=scores[:, tt * 8:tt * 8 + 8],
            scalar1=m1[:, tt:tt + 1], scalar2=None, op0=ALU.is_equal),
            reads=[scb, m1b], writes=[t8b])
        p.emit("dve", lambda E, tt=tt: E.scalar_tensor_tensor(
            out=t8[:, tt * 8:tt * 8 + 8], in0=t8[:, tt * 8:tt * 8 + 8],
            scalar=-1e30, in1=scores[:, tt * 8:tt * 8 + 8],
            op0=ALU.mult, op1=ALU.add), reads=[t8b, scb], writes=[t8b])
    p.emit("dve", lambda E: E.reduce_max(out=m2, in_=t8.rearrange(
        "p (a b) -> p a b", a=16), axis=AX.X), reads=[t8b], writes=[m2b])
    for tt in range(16):
        p.emit("dve", lambda E, tt=tt: E.tensor_scalar(
            out=t8[:, tt * 8:tt * 8 + 8], in0=scores[:, tt * 8:tt * 8 + 8],
            scalar1=m2[:, tt:tt + 1], scalar2=None, op0=ALU.is_ge),
            reads=[scb, m2b], writes=[t8b])
    p.emit("dve", lambda E: E.tensor_mul(out=sig, in0=t8, in1=scores),
           reads=[t8b, scb], writes=[sigb])
    sigbf = c.wb[:, MO_SBF:MO_SBF + 128]
    r0 = wb_r(c, 0, 2048)
    p.emit("dve", lambda E: E.tensor_copy(out=sigbf, in_=sig), reads=[sigb], writes=r0)
    sig8 = c.wb[:, MO_S8:MO_S8 + 2048]
    r1 = wb_r(c, MO_S8, MO_S8 + 2048)
    psbf = c.ps[:].bitcast(BF16)
    for tt in range(16):
        tr_t = ps_tiles(c, 3584, 4096)
        p.emit("pe", lambda E, tt=tt: E.transpose(
            out=psbf[0:8, 7168:7168 + 128],
            in_=sigbf[:, tt * 8:tt * 8 + 8],
            identity=c.st[:, ST_IDENT:ST_IDENT + 128]),
            reads=r0 + [c.misc["ident"]], writes=tr_t)
        p.emit("dve", lambda E, tt=tt: E.tensor_copy(
            out=sig8[0:8, tt * 128:tt * 128 + 128], in_=psbf[0:8, 7168:7168 + 128]),
            reads=ps_tiles(c, 3584, 4096), writes=r1)
    acc_all = [c.ac_b[i] for i in range(8)]
    ws_ctr = [0]

    def stream_wT(ap2d, rows, cols):
        k = rows // 128
        ncols = k * cols
        nslots = (ncols + 2047) // 2048
        s = ws_ctr[0]
        if (s % 5) + nslots > 5:
            s += 5 - (s % 5)
        base = MO_WS + (s % 5) * 2048
        ws_ctr[0] = s + nslots
        sl = c.wb[:, base:base + ncols]
        bufs = wb_r(c, base, base + ncols)
        p.dma("pool", sl.rearrange("p (k d) -> p k d", k=k),
              ap2d.rearrange("(k p) d -> p k d", p=128), writes=bufs)
        return sl, bufs

    for hf in range(2):
        for e in range(-1, E_EXP):
            if e >= 0:
                bc_t = ps_tiles(c, 3072, 4096)
                bcg = Buf(tag + f"bc{hf}_{e}")
                for g in range(2):
                    p.emit("pe", lambda E, g=g, e=e, hf=hf: E.matmul(
                        c.ps[:, 3072 + g * 512:3072 + (g + 1) * 512],
                        c.st[0:8, 4736 + e * 128:4736 + (e + 1) * 128],
                        sig8[0:8, hf * 1024 + g * 512:hf * 1024 + (g + 1) * 512],
                        start=True, stop=True),
                        reads=r1 + [c.misc["esel"]],
                        writes=(bc_t if g == 0 else [bcg]))
                sbc = c.wb[:, MO_BC:MO_BC + 1024]
                bcreg = wb_r(c, MO_BC, MO_BC + 1024)
                p.emit("dve", lambda E: E.tensor_copy(out=sbc, in_=c.ps[:, 3072:4096]),
                       reads=[bcg] + bc_t, writes=bcreg)
            u1reg = wb_r(c, MO_U1, MO_U1 + 8192)
            u3reg = wb_r(c, MO_U3, MO_U3 + 8192)
            for oh4 in range(4):
                for wi, (wap, area, areg) in enumerate(
                        (((sw1T_ap if e < 0 else w1T_ap[e]), MO_U1, u1reg),
                         ((sw3T_ap if e < 0 else w3T_ap[e]), MO_U3, u3reg))):
                    base = ((oh4 * 2 + wi) % 2) * 4
                    mms = []
                    wsl, wbufs = stream_wT(wap[:, oh4 * 256:(oh4 + 1) * 256], D, 256)
                    for kc in range(KC):
                        for oc in range(2):
                            for tg in range(2):
                                ti = base + oc * 2 + tg
                                fn = (lambda E, kc=kc, oc=oc, tg=tg, ti=ti, hf=hf, w=wsl: E.matmul(
                                    c.ps[:, ti * 512:(ti + 1) * 512],
                                    w[:, kc * 256 + oc * 128:kc * 256 + oc * 128 + 128],
                                    c.xn[:, kc * T + hf * 1024 + tg * 512:kc * T + hf * 1024 + (tg + 1) * 512],
                                    start=(kc == 0), stop=(kc == KC - 1)))
                                mms.append((fn, wbufs + [c.xn_b[kc]], kc == 0, kc == KC - 1))
                    grp = mm_group(c, mms, ps_tiles(c, base * 512, (base + 4) * 512),
                                   tag + f"u{wi}{hf}_{e}_{oh4}")
                    for oc in range(2):
                        for tg in range(2):
                            ti = base + oc * 2 + tg
                            ohg = oh4 * 2 + oc
                            dst = c.wb[:, area + ohg * 1024 + tg * 512:area + ohg * 1024 + (tg + 1) * 512]
                            src_ = c.ps[:, ti * 512:(ti + 1) * 512]
                            rd = [grp] + ps_tiles(c, ti * 512, (ti + 1) * 512)
                            if e < 0:
                                if wi == 0:
                                    p.emit("act", lambda E, src_=src_, dst=dst: E.activation(
                                        out=dst, in_=src_, func=AF.Silu), reads=rd, writes=areg)
                                else:
                                    p.emit("act", lambda E, src_=src_, dst=dst: E.activation(
                                        out=dst, in_=src_, func=AF.Copy), reads=rd, writes=areg)
                            else:
                                p.emit("dve", lambda E, src_=src_, dst=dst, tg=tg: E.tensor_mul(
                                    out=dst, in0=src_, in1=sbc[:, tg * 512:(tg + 1) * 512]),
                                    reads=rd + bcreg, writes=areg)
                if e >= 0:
                    p.emit("act", lambda E, oh4=oh4: E.activation(
                        out=c.wb[:, MO_U1 + oh4 * 2048:MO_U1 + (oh4 + 1) * 2048],
                        in_=c.wb[:, MO_U1 + oh4 * 2048:MO_U1 + (oh4 + 1) * 2048],
                        func=AF.Silu), reads=u1reg, writes=u1reg)
                p.emit("dve", lambda E, oh4=oh4: E.tensor_mul(
                    out=c.wb[:, MO_U1 + oh4 * 2048:MO_U1 + (oh4 + 1) * 2048],
                    in0=c.wb[:, MO_U1 + oh4 * 2048:MO_U1 + (oh4 + 1) * 2048],
                    in1=c.wb[:, MO_U3 + oh4 * 2048:MO_U3 + (oh4 + 1) * 2048]),
                    reads=u1reg + u3reg, writes=u1reg)
            w2ap = sw2T_ap if e < 0 else w2T_ap[e]
            for ocg in range(8):
                base = (ocg % 2) * 4
                wsl, wbufs = stream_wT(w2ap[:, ocg * 256:(ocg + 1) * 256], HID, 256)
                mms = []
                for hc in range(8):
                    for oc in range(2):
                        for tg in range(2):
                            ti = base + oc * 2 + tg
                            fn = (lambda E, hc=hc, oc=oc, tg=tg, ti=ti, hf=hf, w=wsl: E.matmul(
                                c.ps[:, ti * 512:(ti + 1) * 512],
                                w[:, hc * 256 + oc * 128:hc * 256 + oc * 128 + 128],
                                c.wb[:, MO_U1 + hc * 1024 + tg * 512:MO_U1 + hc * 1024 + (tg + 1) * 512],
                                start=(hc == 0), stop=(hc == 7)))
                            mms.append((fn, wbufs + u1reg, hc == 0, hc == 7))
                grp = mm_group(c, mms, ps_tiles(c, base * 512, (base + 4) * 512),
                               tag + f"y{hf}_{e}_{ocg}")
                for oc in range(2):
                    for tg in range(2):
                        ti = base + oc * 2 + tg
                        ocglob = ocg * 2 + oc
                        dst = c.ac[:, ocglob * 1024 + tg * 512:ocglob * 1024 + (tg + 1) * 512]
                        src_ = c.ps[:, ti * 512:(ti + 1) * 512]
                        rd = [grp] + ps_tiles(c, ti * 512, (ti + 1) * 512)
                        ar = [c.ac_b[ocglob // 2]]
                        if e < 0:
                            p.emit("dve", lambda E, src_=src_, dst=dst: E.tensor_copy(
                                out=dst, in_=src_), reads=rd + ar, writes=ar)
                        else:
                            p.emit("dve", lambda E, src_=src_, dst=dst: E.tensor_add(
                                out=dst, in0=dst, in1=src_), reads=rd + ar, writes=ar)
        for oc in range(16):
            p.dma("pool", xdst_ap[oc * 128:(oc + 1) * 128, hf * 1024:(hf + 1) * 1024],
                  c.ac[:, oc * 1024:(oc + 1) * 1024], accum_op=ALU.add,
                  reads=[c.ac_b[oc // 2], xdst_b], writes=[xdst_b])


def emit_summary_b(c, bktm, bE, bv, akb_ap, avb_ap, srb_ap):
    p = c.p
    kbb = [Buf("sbk0"), Buf("sbk1")]
    ebb = [Buf("sbe0"), Buf("sbe1")]
    vbb = [Buf("sbv0"), Buf("sbv1")]
    ak_t = ps_tiles(c, 0, 1024)
    av_t = ps_tiles(c, 1024, 3072)
    sr_t = ps_tiles(c, 3072, 4096)
    grp = Buf("sbgrp")
    for tt in range(16):
        r = tt % 2
        ks = c.xn[:, r * 1024:r * 1024 + 1024]
        es = c.xn[:, 2048 + r * 1024:2048 + r * 1024 + 1024]
        vs = c.xn[:, 4096 + r * 2048:4096 + r * 2048 + 2048]
        p.dma("sp", ks, bktm[tt], reads=[c.misc["d_bktm"]],
              writes=[kbb[r]] + xn_r(c, r * 1024, r * 1024 + 1024))
        p.dma("sp", es, bE[tt], reads=[c.misc["d_bE"]],
              writes=[ebb[r]] + xn_r(c, 2048 + r * 1024, 2048 + r * 1024 + 1024))
        p.dma("sp", vs, bv[tt], reads=[c.misc["d_bv"]],
              writes=[vbb[r]] + xn_r(c, 4096 + r * 2048, 4096 + r * 2048 + 2048))
        first = (tt == 0)
        last = (tt == 15)
        for h in range(B_H):
            wr = (ak_t + av_t + sr_t) if (first and h == 0) else []
            p.emit("pe", lambda E, h=h, ks=ks, es=es, first=first, last=last: E.matmul(
                c.ps[0:64, h * 64:h * 64 + 64],
                ks[:, h * 64:h * 64 + 64], es[:, h * 64:h * 64 + 64],
                start=first, stop=last),
                reads=[kbb[r], ebb[r]], writes=wr)
            p.emit("pe", lambda E, h=h, es=es, vs=vs, first=first, last=last: E.matmul(
                c.ps[0:64, 1024 + h * 128:1024 + h * 128 + 128],
                es[:, h * 64:h * 64 + 64], vs[:, h * 128:h * 128 + 128],
                start=first, stop=last),
                reads=[ebb[r], vbb[r]], writes=[])
        for g in range(2):
            wr = [grp] if (last and g == 1) else []
            p.emit("pe", lambda E, g=g, es=es, first=first, last=last: E.matmul(
                c.ps[0:1, 3072 + g * 512:3072 + (g + 1) * 512],
                c.st[:, ST_ONES_COL:ST_ONES_COL + 1],
                es[:, g * 512:(g + 1) * 512], start=first, stop=last),
                reads=[ebb[r], c.misc["ones_col"]], writes=wr)
    eb = Buf("sbev")
    p.emit("dve", lambda E: E.tensor_copy(out=c.ac[0:64, 0:3072], in_=c.ps[0:64, 0:3072]),
           reads=[grp] + ak_t + av_t, writes=ac_r(c, 0, 3072) + [eb])
    eb2 = Buf("sbev2")
    p.emit("dve", lambda E: E.tensor_copy(out=c.ac[0:1, 3072:4096], in_=c.ps[0:1, 3072:4096]),
           reads=[grp] + sr_t, writes=[eb2])
    p.dma("sp", akb_ap, c.ac[0:64, 0:1024], reads=[eb], writes=[Buf("o1")])
    p.dma("sp", avb_ap, c.ac[0:64, 1024:3072], reads=[eb], writes=[Buf("o2")])
    p.dma("sp", srb_ap, c.ac[0:1, 3072:4096], reads=[eb2], writes=[Buf("o3")])


def build_l2():
    nc = bass.Bass("TRN2", target_bir_lowering=False, debug=False)
    DT = lambda n, s, d, k: nc.dram_tensor(n, s, d, kind=k).ap()
    xT = DT("xT", [D, T], F32, "ExternalInput")
    xn1 = DT("xn1", [16, 128, T], BF16, "ExternalInput")
    Wq = DT("Wq", [D, D], F32, "ExternalInput")
    Wk = DT("Wk", [2176, D], F32, "ExternalInput")
    aktm_in = DT("aktm", [16, 128, D], BF16, "ExternalInput")
    av_in = DT("av", [16, 128, D], BF16, "ExternalInput")
    sall = DT("sall", [8, 2048, 256], F32, "ExternalInput")
    cbigA = DT("cbigA", [128, 6144], F32, "ExternalInput")
    gwT_r = DT("gwT_r", [128, 128], F32, "ExternalInput")
    w1T = DT("w1T", [E_EXP, D, HID], F32, "ExternalInput")
    w3T = DT("w3T", [E_EXP, D, HID], F32, "ExternalInput")
    w2T = DT("w2T", [E_EXP, HID, D], F32, "ExternalInput")
    sw1T = DT("sw1T", [D, HID], F32, "ExternalInput")
    sw3T = DT("sw3T", [D, HID], F32, "ExternalInput")
    sw2T = DT("sw2T", [HID, D], F32, "ExternalInput")
    WqB = DT("WqB", [D, HID], F32, "ExternalInput")
    WkB = DT("WkB", [D, HID], F32, "ExternalInput")
    WsB = DT("WsB", [D, HID], F32, "ExternalInput")
    WvB = DT("WvB", [D, D], F32, "ExternalInput")
    WgB = DT("WgB", [D, D], F32, "ExternalInput")
    csm = DT("csm", [128, 960], F32, "ExternalInput")
    cbf = DT("cbf", [128, 1280], BF16, "ExternalInput")
    xres = DT("xres", [D, T], F32, "ExternalOutput")
    akb = DT("akb", [64, 1024], F32, "ExternalOutput")
    avb = DT("avb", [64, 2048], F32, "ExternalOutput")
    srb = DT("srb", [1, 1024], F32, "ExternalOutput")
    xn3 = DT("xn3", [16, 128, T], BF16, "ExternalOutput")
    bq = DT("bq", [8, 128, T], BF16, "ExternalOutput")
    bkfm = DT("bkfm", [8, 128, T], BF16, "ExternalOutput")
    bktm = DT("bktm", [16, 128, HID], BF16, "ExternalOutput")
    bE = DT("bE", [16, 128, HID], BF16, "ExternalOutput")
    bv = DT("bv", [16, 128, D], BF16, "ExternalOutput")
    bg = DT("bg", [16, 128, D], BF16, "ExternalOutput")
    aq = DT("aq", [16, 128, T], BF16, "Internal")
    akfm = DT("akfm", [16, 128, T], BF16, "Internal")

    es = ExitStack()
    with es:
        p = Prog(nc)
        c = make_ctx(nc, p, es)
        emit_init(c, csm, cbf)
        for kc in range(KC):
            p.dma("sp", c.xn[:, kc * T:(kc + 1) * T], xn1[kc],
                  writes=[c.xn_b[kc]])
        emit_proj(c, Wq, D, "fm", aq, bias=("col", SM_BIASQ), tag="q",
                  spill_buf=c.misc["d_aq"])
        emit_proj(c, Wk, D, "fm", akfm, bias=("col", SM_BIASK), tag="kf",
                  spill_buf=c.misc["d_akfm"])
        emit_combine_a(c, sall)
        emit_scan_a(c, aq, akfm, aktm_in, av_in, cbigA, xT, xres,
                    c.misc["d_xT"], c.misc["d_xres"])
        emit_ln(c, xres, "l2b", xsrc_b=c.misc["d_xres"])
        emit_moe(c, gwT_r, w1T, w3T, w2T, sw1T, sw3T, sw2T, xres,
                 c.misc["d_xres"], tag="m1")
        emit_ln(c, xres, "l2c", xsrc_b=c.misc["d_xres"])
        for kc in range(KC):
            p.dma("sp", xn3[kc], c.xn[:, kc * T:(kc + 1) * T],
                  reads=[c.xn_b[kc]], writes=[c.misc["d_xn3"]])
        emit_proj(c, WqB, HID, "fm", bq, scale=0.125, tag="bq",
                  spill_buf=c.misc["d_bq"])
        emit_proj(c, WkB, HID, "fm", bkfm, tag="bkf", spill_buf=c.misc["d_bkfm"])
        emit_proj(c, WkB, HID, "tm", bktm, tag="bk", spill_buf=c.misc["d_bktm"])
        emit_proj(c, WsB, HID, "tm", bE, act="expclip", tag="bs",
                  spill_buf=c.misc["d_bE"])
        emit_proj(c, WvB, D, "tm", bv, tag="bv", spill_buf=c.misc["d_bv"])
        emit_proj(c, WgB, D, "tm", bg, act="silu", tag="bg", spill_buf=c.misc["d_bg"])
        emit_summary_b(c, bktm, bE, bv, akb, avb, srb)
        p.finalize(es)
    return nc


def build_dbg_fm():
    nc = bass.Bass("TRN2", target_bir_lowering=False, debug=False)
    DT = lambda n, s, d, k: nc.dram_tensor(n, s, d, kind=k).ap()
    xT = DT("xT", [D, T], F32, "ExternalInput")
    Wq = DT("Wq", [D, D], F32, "ExternalInput")
    csm = DT("csm", [128, 960], F32, "ExternalInput")
    cbf = DT("cbf", [128, 1280], BF16, "ExternalInput")
    aq = DT("aq", [16, 128, T], BF16, "ExternalOutput")
    es = ExitStack()
    with es:
        p = Prog(nc)
        c = make_ctx(nc, p, es)
        emit_init(c, csm, cbf)
        emit_ln(c, xT, "d")
        emit_proj(c, Wq, D, "fm", aq, bias=("col", SM_BIASQ), tag="q",
                  spill_buf=c.misc["d_aq"])
        p.finalize(es)
    return nc


def build_dbg_noscan():
    nc = bass.Bass("TRN2", target_bir_lowering=False, debug=False)
    DT = lambda n, s, d, k: nc.dram_tensor(n, s, d, kind=k).ap()
    xT = DT("xT", [D, T], F32, "ExternalInput")
    Wq = DT("Wq", [D, D], F32, "ExternalInput")
    Wk = DT("Wk", [2176, D], F32, "ExternalInput")
    Wv = DT("Wv", [2176, D], F32, "ExternalInput")
    csm = DT("csm", [128, 960], F32, "ExternalInput")
    cbf = DT("cbf", [128, 1280], BF16, "ExternalInput")
    aq = DT("aq", [16, 128, T], BF16, "ExternalOutput")
    akfm = DT("akfm", [16, 128, T], BF16, "ExternalOutput")
    aktm = DT("aktm", [16, 128, D], BF16, "ExternalOutput")
    av = DT("av", [16, 128, D], BF16, "ExternalOutput")
    es = ExitStack()
    with es:
        p = Prog(nc)
        c = make_ctx(nc, p, es)
        emit_init(c, csm, cbf)
        emit_ln(c, xT, "l2a")
        emit_proj(c, Wq, D, "fm", aq, bias=("col", SM_BIASQ), tag="q",
                  spill_buf=c.misc["d_aq"])
        emit_proj(c, Wk, D, "fm", akfm, bias=("col", SM_BIASK), tag="kf",
                  spill_buf=c.misc["d_akfm"])
        emit_proj(c, Wk, D, "tm", aktm, bias=("w17",), tag="kt",
                  spill_buf=c.misc["d_aktm"])
        emit_proj(c, Wv, D, "tm", av, bias=("w17",), tag="vt",
                  spill_buf=c.misc["d_av"])
        p.finalize(es)
    return nc


def build_dbg_scan():
    nc = bass.Bass("TRN2", target_bir_lowering=False, debug=False)
    DT = lambda n, s, d, k: nc.dram_tensor(n, s, d, kind=k).ap()
    xT = DT("xT", [D, T], F32, "ExternalInput")
    xn1 = DT("xn1", [16, 128, T], BF16, "ExternalInput")
    Wq = DT("Wq", [D, D], F32, "ExternalInput")
    Wk = DT("Wk", [2176, D], F32, "ExternalInput")
    aktm_in = DT("aktm", [16, 128, D], BF16, "ExternalInput")
    av_in = DT("av", [16, 128, D], BF16, "ExternalInput")
    sall = DT("sall", [8, 2048, 256], F32, "ExternalInput")
    cbigA = DT("cbigA", [128, 6144], F32, "ExternalInput")
    csm = DT("csm", [128, 960], F32, "ExternalInput")
    cbf = DT("cbf", [128, 1280], BF16, "ExternalInput")
    xres = DT("xres", [D, T], F32, "ExternalOutput")
    aq = DT("aq", [16, 128, T], BF16, "ExternalOutput")
    akfm = DT("akfm", [16, 128, T], BF16, "ExternalOutput")
    aktm = DT("aktm", [16, 128, D], BF16, "ExternalOutput")
    av = DT("av", [16, 128, D], BF16, "ExternalOutput")
    es = ExitStack()
    with es:
        p = Prog(nc)
        c = make_ctx(nc, p, es)
        emit_init(c, csm, cbf)
        emit_ln(c, xT, "l2a")
        emit_proj(c, Wq, D, "fm", aq, bias=("col", SM_BIASQ), tag="q",
                  spill_buf=c.misc["d_aq"])
        emit_proj(c, Wk, D, "fm", akfm, bias=("col", SM_BIASK), tag="kf",
                  spill_buf=c.misc["d_akfm"])
        emit_proj(c, Wk, D, "tm", aktm, bias=("w17",), tag="kt",
                  spill_buf=c.misc["d_aktm"])
        emit_proj(c, Wv, D, "tm", av, bias=("w17",), tag="vt",
                  spill_buf=c.misc["d_av"])
        emit_combine_a(c, sall)
        emit_scan_a(c, aq, akfm, aktm, av, cbigA, xT, xres,
                    c.misc["d_xT"], c.misc["d_xres"])
        p.finalize(es)
    return nc


def emit_combine_b(c, akall, avall, srall):
    p = c.p
    smb = c.misc["sm"]
    Ak = c.ac[0:64, 0:1024]
    Av = c.ac[0:64, 1024:3072]
    Sp = c.ac[0:1, 3072:4096]
    sreg = ac_r(c, 0, 4096)
    p.emit("pool", lambda E: E.memset(c.ac[:, 0:4096], 0.0), writes=sreg, inc=1)
    streg = ac_r(c, 4096, 8192)
    for j in range(8):
        p.dma("sp", c.ac[0:64, 4096:5120], akall[j], writes=[streg[0]])
        p.dma("sp", c.ac[0:64, 5120:7168], avall[j], writes=[streg[1]])
        p.dma("sp", c.ac[0:1, 7168:8192], srall[j], writes=[Buf(f"cbs{j}")] + [streg[1]])
        awc = c.sm[0:64, SM_AW01 + j:SM_AW01 + j + 1]
        p.emit("dve", lambda E, j=j, awc=awc: E.scalar_tensor_tensor(
            out=Ak, in0=c.ac[0:64, 4096:5120], scalar=awc, in1=Ak,
            op0=ALU.mult, op1=ALU.add), reads=streg + [smb] + sreg, writes=sreg)
        p.emit("dve", lambda E, j=j, awc=awc: E.scalar_tensor_tensor(
            out=Av, in0=c.ac[0:64, 5120:7168], scalar=awc, in1=Av,
            op0=ALU.mult, op1=ALU.add), reads=streg + [smb] + sreg, writes=sreg)
        p.emit("dve", lambda E, j=j, awc=awc: E.scalar_tensor_tensor(
            out=Sp, in0=c.ac[0:1, 7168:8192], scalar=awc[0:1, :], in1=Sp,
            op0=ALU.mult, op1=ALU.add), reads=streg + [smb] + sreg, writes=sreg)


SCANB_STEPS = [99]
SCANB_CHUNKS = [B_NCH]


def emit_scan_b(c, bq, bkfm, bktm, bE, bv, bg, cbigB_ap, gnw_ap, bo, tag="sb"):
    STEPS = SCANB_STEPS[0]
    NCHS = SCANB_CHUNKS[0]
    p = c.p
    nc = c.nc
    smb = c.misc["sm"]
    p.dma("sp", c.ac[:, 4096:6144], gnw_ap, writes=ac_r(c, 4096, 6144))
    p.dma("sp", c.ac[:, 6144:8192], cbigB_ap, writes=ac_r(c, 6144, 8192))
    mkreg = ac_r(c, 4096, 8448)
    sreg = ac_r(c, 0, 4096)
    wbst = wb_r(c, 0, 4096)
    p.emit("dve", lambda E: E.tensor_copy(
           out=c.wb[0:64, 0:1024], in_=c.ac[0:64, 0:1024]), reads=sreg, writes=wbst)
    p.emit("dve", lambda E: E.tensor_copy(out=c.wb[0:64, 1024:3072],
           in_=c.ac[0:64, 1024:3072]), reads=sreg, writes=wbst)
    p.emit("dve", lambda E: E.tensor_copy(out=c.wb[0:1, 3072:4096],
           in_=c.ac[0:1, 3072:4096]), reads=sreg, writes=wbst)
    psbf = c.ps[:].bitcast(BF16)
    scr2 = wb_r(c, 4096, 8192)
    scr3 = wb_r(c, 8192, 12288)
    scr4 = wb_r(c, 12288, 13312)
    strm = [[Buf(f"{tag}s{i}{r}") for r in range(2)] for i in range(6)]
    for n in range(NCHS):
        r = n % 2
        qs = c.xn[0:64, r * 2048:r * 2048 + 2048]
        kfs = c.xn[0:64, 4096 + r * 2048:4096 + r * 2048 + 2048]
        kts = c.xn[:, 8192 + r * 1024:8192 + r * 1024 + 1024]
        es_ = c.xn[:, 10240 + r * 1024:10240 + r * 1024 + 1024]
        vs = c.xn[:, 12288 + r * 2048:12288 + r * 2048 + 2048]
        gs = c.xn[:, 16384 + r * 2048:16384 + r * 2048 + 2048]
        p.dma("sp", qs.rearrange("p (y d) -> p y d", y=16),
              bq.rearrange("x (s p) d -> p (x s) d", s=2)[:, :, n * 128:(n + 1) * 128],
              reads=[c.misc["d_bq"]], writes=[strm[0][r]] + xn_r(c, r * 2048, r * 2048 + 2048))
        p.dma("sp", kfs.rearrange("p (y d) -> p y d", y=16),
              bkfm.rearrange("x (s p) d -> p (x s) d", s=2)[:, :, n * 128:(n + 1) * 128],
              reads=[c.misc["d_bkfm"]], writes=[strm[1][r]] + xn_r(c, 4096 + r * 2048, 4096 + r * 2048 + 2048))
        p.dma("sp", kts, bktm[n], reads=[c.misc["d_bktm"]],
              writes=[strm[2][r]] + xn_r(c, 8192 + r * 1024, 8192 + r * 1024 + 1024))
        p.dma("sp", es_, bE[n], reads=[c.misc["d_bE"]],
              writes=[strm[3][r]] + xn_r(c, 10240 + r * 1024, 10240 + r * 1024 + 1024))
        p.dma("sp", vs, bv[n], reads=[c.misc["d_bv"]],
              writes=[strm[4][r]] + xn_r(c, 12288 + r * 2048, 12288 + r * 2048 + 2048))
        p.dma("sp", gs, bg[n], reads=[c.misc["d_bg"]],
              writes=[strm[5][r]] + xn_r(c, 16384 + r * 2048, 16384 + r * 2048 + 2048))
        if STEPS < 1:
            continue
        mms = []
        for h in range(B_H):
            fn = (lambda E, h=h, kfs=kfs, qs=qs: E.matmul(
                c.ps[:, h * 128:(h + 1) * 128],
                kfs[:, h * 128:(h + 1) * 128],
                qs[:, h * 128:(h + 1) * 128],
                start=True, stop=True))
            mms.append((fn, [strm[0][r], strm[1][r]], True, True))
        kqg = mm_group(c, mms, ps_tiles(c, 0, 2048), tag + f"kq{n}")
        KQm = c.wb[:, 4096:6144]
        p.emit("dve", lambda E: E.tensor_mul(out=KQm, in0=c.ps[:, 0:2048],
               in1=c.ac[:, 6144:8192]), reads=[kqg] + ps_tiles(c, 0, 2048) + mkreg,
               writes=scr2)
        if STEPS < 2:
            continue
        mms = []
        for g in range(2):
            fn = (lambda E, g=g, es_=es_: E.matmul(
                c.ps[:, 2048 + g * 512:2048 + (g + 1) * 512],
                c.st[:, 5760:5888],
                es_[:, g * 512:(g + 1) * 512], start=True, stop=False))
            mms.append((fn, [strm[3][r], c.misc["lt"]], g == 0, False))
        for g in range(2):
            fn = (lambda E, g=g: E.matmul(
                c.ps[:, 2048 + g * 512:2048 + (g + 1) * 512],
                c.st[0:1, ST_ONES_ROW:ST_ONES_ROW + 128],
                c.wb[0:1, 3072 + g * 512:3072 + (g + 1) * 512],
                start=False, stop=True))
            mms.append((fn, wbst + [c.misc["ones_row"]], False, True))
        scg = mm_group(c, mms, ps_tiles(c, 2048, 3072), tag + f"sc{n}")
        mms = []
        for g in range(2):
            fn = (lambda E, g=g, es_=es_: E.matmul(
                c.ps[0:1, 3072 + g * 512:3072 + (g + 1) * 512],
                c.st[:, ST_ONES_COL:ST_ONES_COL + 1],
                es_[:, g * 512:(g + 1) * 512], start=True, stop=True))
            mms.append((fn, [strm[3][r], c.misc["ones_col"]], True, True))
        csg = mm_group(c, mms, ps_tiles(c, 3072, 4096), tag + f"cs{n}")
        Sinv = c.ac[:, 12288:13312]
        sinvreg = ac_r(c, 12288, 13312)
        p.emit("dve", lambda E: E.reciprocal(out=Sinv, in_=c.ps[:, 2048:3072]),
               reads=[scg] + ps_tiles(c, 2048, 3072), writes=sinvreg)
        if STEPS < 3:
            continue
        mms = []
        for h in range(B_H):
            fn = (lambda E, h=h, es_=es_: E.matmul(
                c.ps[:, h * 64:(h + 1) * 64],
                KQm[:, h * 128:(h + 1) * 128],
                es_[:, h * 64:(h + 1) * 64], start=True, stop=False))
            mms.append((fn, scr2 + [strm[3][r]], h == 0, False))
            fn2 = (lambda E, h=h, qs=qs: E.matmul(
                c.ps[:, h * 64:(h + 1) * 64],
                qs[:, h * 128:(h + 1) * 128],
                c.wb[0:64, h * 64:h * 64 + 64], start=False, stop=True))
            mms.append((fn2, [strm[0][r]] + wbst, False, True))
        okg = mm_group(c, mms, ps_tiles(c, 0, 1024), tag + f"ok{n}")
        okb = c.ac[:, 13312:14336]
        okreg = ac_r(c, 13312, 14336)
        p.emit("dve", lambda E: E.tensor_mul(out=okb, in0=c.ps[:, 0:1024], in1=Sinv),
               reads=[okg] + ps_tiles(c, 0, 1024) + sinvreg, writes=okreg)
        if STEPS < 4:
            continue
        nm = c.ac[:, 8320:8336]
        rs = c.ac[:, 8336:8352]
        streg = ac_r(c, 8320, 8352)
        p.emit("dve", lambda E: E.reduce_max(out=nm, in_=okb.rearrange(
            "p (a b) -> p a b", a=16), axis=AX.X, negate=True),
            reads=okreg, writes=streg)
        for h in range(B_H):
            p.emit("act", lambda E, h=h: E.activation(
                out=okb[:, h * 64:(h + 1) * 64], in_=okb[:, h * 64:(h + 1) * 64],
                func=AF.Exp, bias=nm[:, h:h + 1]), reads=okreg + streg, writes=okreg)
        p.emit("dve", lambda E: E.reduce_sum(out=rs, in_=okb.rearrange(
            "p (a b) -> p a b", a=16), axis=AX.X), reads=okreg, writes=streg)
        p.emit("dve", lambda E: E.reciprocal(out=rs, in_=rs), reads=streg, writes=streg)
        for h in range(B_H):
            p.emit("dve", lambda E, h=h: E.scalar_tensor_tensor(
                out=okb[:, h * 64:(h + 1) * 64], in0=okb[:, h * 64:(h + 1) * 64],
                scalar=rs[:, h:h + 1], in1=Sinv[:, h * 64:(h + 1) * 64],
                op0=ALU.mult, op1=ALU.mult), reads=okreg + streg + sinvreg,
                writes=okreg)
        Wbf = c.wb[:, 12288:13312]
        p.emit("dve", lambda E: E.tensor_copy(out=Wbf, in_=okb), reads=okreg, writes=scr4)
        if STEPS < 5:
            continue
        ET = c.wb[0:64, 8192:10240]
        WT = c.wb[0:64, 10240:12288]
        trt = ps_tiles(c, 1024, 2048)
        etdone = Buf(tag + f"et{n}")
        for h in range(B_H):
            wr = list(trt) if h == 0 else []
            if h == B_H - 1:
                wr.append(etdone)
            p.emit("pe", lambda E, h=h, es_=es_: E.transpose(
                out=psbf[0:64, 2048 + h * 128:2048 + (h + 1) * 128],
                in_=es_[:, h * 64:(h + 1) * 64],
                identity=c.st[:, ST_IDENT:ST_IDENT + 128]),
                reads=[strm[3][r], c.misc["ident"]], writes=wr)
        p.emit("dve", lambda E: E.tensor_copy(out=ET, in_=psbf[0:64, 2048:4096]),
               reads=[etdone] + trt, writes=scr3)
        wtdone = Buf(tag + f"wt{n}")
        for h in range(B_H):
            wr = list(trt) if h == 0 else []
            if h == B_H - 1:
                wr.append(wtdone)
            p.emit("pe", lambda E, h=h: E.transpose(
                out=psbf[0:64, 2048 + h * 128:2048 + (h + 1) * 128],
                in_=Wbf[:, h * 64:(h + 1) * 64],
                identity=c.st[:, ST_IDENT:ST_IDENT + 128]),
                reads=scr4 + [c.misc["ident"]], writes=wr)
        p.emit("dve", lambda E: E.tensor_copy(out=WT, in_=psbf[0:64, 2048:4096]),
               reads=[wtdone] + trt, writes=scr3)
        if STEPS < 6:
            continue
        mms = []
        for h in range(B_H):
            fn = (lambda E, h=h: E.matmul(
                c.ps[:, h * 128:(h + 1) * 128],
                ET[:, h * 128:(h + 1) * 128],
                WT[:, h * 128:(h + 1) * 128], start=True, stop=True))
            mms.append((fn, scr3, True, True))
        ptg = mm_group(c, mms, ps_tiles(c, 0, 2048), tag + f"pt{n}")
        PTm = c.wb[:, 6144:8192]
        p.emit("dve", lambda E: E.tensor_mul(out=PTm, in0=c.ps[:, 0:2048],
               in1=c.ac[:, 6144:8192]), reads=[ptg] + ps_tiles(c, 0, 2048) + mkreg,
               writes=scr2)
        if STEPS < 7:
            continue
        mms = []
        for h in range(B_H):
            fn = (lambda E, h=h, vs=vs: E.matmul(
                c.ps[:, h * 128:(h + 1) * 128],
                PTm[:, h * 128:(h + 1) * 128],
                vs[:, h * 128:(h + 1) * 128], start=True, stop=False))
            mms.append((fn, scr2 + [strm[4][r]], h == 0, False))
            fn2 = (lambda E, h=h: E.matmul(
                c.ps[:, h * 128:(h + 1) * 128],
                WT[:, h * 128:(h + 1) * 128],
                c.wb[0:64, 1024 + h * 128:1024 + (h + 1) * 128],
                start=False, stop=True))
            mms.append((fn2, scr3 + wbst, False, True))
        og = mm_group(c, mms, ps_tiles(c, 0, 2048), tag + f"o{n}")
        if STEPS < 8:
            continue
        o2 = c.ac[:, 10240:12288]
        o2reg = ac_r(c, 10240, 12288)
        p.emit("act", lambda E: E.activation(out=o2, in_=c.ps[:, 0:2048],
               func=AF.Square), reads=[og] + ps_tiles(c, 0, 2048), writes=o2reg)
        rstd = c.ac[:, 8352:8368]
        p.emit("dve", lambda E: E.reduce_sum(out=rstd, in_=o2.rearrange(
            "p (a b) -> p a b", a=16), axis=AX.X), reads=o2reg, writes=streg)
        p.emit("dve", lambda E: E.tensor_scalar(out=rstd, in0=rstd,
               scalar1=1.0 / B_DV, scalar2=B_EPS, op0=ALU.mult, op1=ALU.add),
               reads=streg, writes=streg)
        p.emit("dve", lambda E: E.reciprocal(out=rstd, in_=rstd), reads=streg, writes=streg)
        p.emit("act", lambda E: E.activation(out=rstd, in_=rstd, func=AF.Sqrt),
               reads=streg, writes=streg)
        for h in range(B_H):
            p.emit("dve", lambda E, h=h: E.tensor_scalar(
                out=o2[:, h * 128:(h + 1) * 128], in0=c.ps[:, h * 128:(h + 1) * 128],
                scalar1=rstd[:, h:h + 1], scalar2=None, op0=ALU.mult),
                reads=[og] + ps_tiles(c, h * 128, (h + 1) * 128) + streg, writes=o2reg)
        p.emit("dve", lambda E: E.tensor_mul(out=o2, in0=o2, in1=c.ac[:, 4096:6144]),
               reads=o2reg + mkreg, writes=o2reg)
        stg = c.st[:, (n % 2) * 2048:(n % 2) * 2048 + 2048]
        sb_ = c.st_b[n % 2]
        p.emit("dve", lambda E, gs=gs, stg=stg: E.tensor_mul(out=stg, in0=o2, in1=gs),
               reads=o2reg + [strm[5][r]], writes=[sb_])
        p.dma("sp", bo[n], stg, reads=[sb_], writes=[c.misc["d_bo"]])
        if STEPS < 9:
            continue
        mms = []
        for h in range(B_H):
            fn = (lambda E, h=h, kts=kts, es_=es_: E.matmul(
                c.ps[0:64, 2048 + h * 64:2048 + h * 64 + 64],
                kts[:, h * 64:(h + 1) * 64], es_[:, h * 64:(h + 1) * 64],
                start=True, stop=True))
            mms.append((fn, [strm[2][r], strm[3][r]], True, True))
        akg = mm_group(c, mms, ps_tiles(c, 2048, 3072), tag + f"ak{n}")
        mms = []
        for h in range(B_H):
            fn = (lambda E, h=h, es_=es_, vs=vs: E.matmul(
                c.ps[0:64, h * 128:(h + 1) * 128],
                es_[:, h * 64:(h + 1) * 64], vs[:, h * 128:(h + 1) * 128],
                start=True, stop=True))
            mms.append((fn, [strm[3][r], strm[4][r]], True, True))
        avg = mm_group(c, mms, ps_tiles(c, 0, 2048), tag + f"av{n}")
        p.emit("dve", lambda E: E.tensor_add(out=c.ac[0:64, 0:1024],
               in0=c.ac[0:64, 0:1024], in1=c.ps[0:64, 2048:3072]),
               reads=[akg] + ps_tiles(c, 2048, 3072) + sreg, writes=sreg)
        p.emit("dve", lambda E: E.tensor_add(out=c.ac[0:64, 1024:3072],
               in0=c.ac[0:64, 1024:3072], in1=c.ps[0:64, 0:2048]),
               reads=[avg] + ps_tiles(c, 0, 2048) + sreg, writes=sreg)
        p.emit("dve", lambda E: E.tensor_add(out=c.ac[0:1, 3072:4096],
               in0=c.ac[0:1, 3072:4096], in1=c.ps[0:1, 3072:4096]),
               reads=[csg] + ps_tiles(c, 3072, 4096) + sreg, writes=sreg)
        if n < B_NCH - 1:
            p.emit("dve", lambda E: E.tensor_copy(out=c.wb[0:64, 0:1024],
                   in_=c.ac[0:64, 0:1024]), reads=sreg, writes=wbst)
            p.emit("dve", lambda E: E.tensor_copy(out=c.wb[0:64, 1024:3072],
                   in_=c.ac[0:64, 1024:3072]), reads=sreg, writes=wbst)
            p.emit("dve", lambda E: E.tensor_copy(out=c.wb[0:1, 3072:4096],
                   in_=c.ac[0:1, 3072:4096]), reads=sreg, writes=wbst)


def emit_wo(c, Wo_ap, bo, xsrc_ap, xsrc_b, ydst_ap, ydst_b, tag="wo"):
    p = c.p
    psbf = c.ps[:].bitcast(BF16)
    for kc in range(KC):
        slot = c.wb[:, kc * 2048:kc * 2048 + 2048]
        p.dma("pool", slot, Wo_ap[kc * 128:(kc + 1) * 128, :], writes=[c.wb_b[kc]])
    ob = [Buf(tag + "o0"), Buf(tag + "o1")]
    fm = [Buf(tag + "f0"), Buf(tag + "f1")]
    xb = [Buf(tag + "x0"), Buf(tag + "x1")]
    for gi in range(4):
        r = gi % 2
        bos = c.xn[:, r * 8192:r * 8192 + 8192]
        p.dma("sp", bos.rearrange("p (x d) -> p x d", x=4), bo[gi * 4:gi * 4 + 4].rearrange("x p d -> p x d"),
              reads=[c.misc["d_bo"]], writes=[ob[r]] + xn_r(c, r * 8192, r * 8192 + 8192))
        ofm = c.xn[:, 16384 + r * 8192:16384 + r * 8192 + 8192]
        ofm_reg = xn_r(c, 16384 + r * 8192, 16384 + r * 8192 + 8192)
        trt = ps_tiles(c, 2048, 3072)
        for x in range(16):
            tg_ = Buf(tag + f"t{gi}_{x}")
            for cc in range(4):
                wr = list(trt) if cc == 0 else []
                if cc == 3:
                    wr.append(tg_)
                p.emit("pe", lambda E, x=x, cc=cc, bos=bos: E.transpose(
                    out=psbf[:, 4096 + cc * 128:4096 + (cc + 1) * 128],
                    in_=bos[:, cc * 2048 + x * 128:cc * 2048 + (x + 1) * 128],
                    identity=c.st[:, ST_IDENT:ST_IDENT + 128]),
                    reads=[ob[r], c.misc["ident"]], writes=wr)
            p.emit("dve", lambda E, x=x, ofm=ofm: E.tensor_copy(
                out=ofm[:, x * 512:(x + 1) * 512], in_=psbf[:, 4096:4608]),
                reads=[tg_] + trt, writes=ofm_reg)
        for oc2 in range(8):
            mms = []
            for kc in range(KC):
                for oc in range(2):
                    ti = oc
                    fn = (lambda E, kc=kc, oc=oc, oc2=oc2, ofm=ofm: E.matmul(
                        c.ps[:, oc * 512:(oc + 1) * 512],
                        c.wb[:, kc * 2048 + oc2 * 256 + oc * 128:kc * 2048 + oc2 * 256 + (oc + 1) * 128],
                        ofm[:, kc * 512:(kc + 1) * 512],
                        start=(kc == 0), stop=(kc == KC - 1)))
                    mms.append((fn, [c.wb_b[kc]] + ofm_reg, kc == 0, kc == KC - 1))
            grp = mm_group(c, mms, ps_tiles(c, 0, 1024), tag + f"g{gi}_{oc2}")
            xs = c.ac[:, r * 2048:r * 2048 + 1024]
            xreg = ac_r(c, r * 2048, r * 2048 + 1024)
            p.dma("sp", xs.rearrange("p (o d) -> p o d", o=2),
                  xsrc_ap[oc2 * 256:(oc2 + 1) * 256, gi * 512:(gi + 1) * 512].rearrange(
                      "(o p) d -> p o d", p=128),
                  reads=[xsrc_b], writes=xb[r] if False else xreg)
            for oc in range(2):
                p.emit("dve", lambda E, oc=oc, xs=xs: E.tensor_add(
                    out=xs[:, oc * 512:(oc + 1) * 512], in0=xs[:, oc * 512:(oc + 1) * 512],
                    in1=c.ps[:, oc * 512:(oc + 1) * 512]),
                    reads=[grp] + ps_tiles(c, oc * 512, (oc + 1) * 512) + xreg, writes=xreg)
            p.dma("sp", ydst_ap[oc2 * 256:(oc2 + 1) * 256, gi * 512:(gi + 1) * 512].rearrange(
                  "(o p) d -> p o d", p=128), xs.rearrange("p (o d) -> p o d", o=2),
                  reads=xreg, writes=[ydst_b])


def build_l3(stop_after="full"):
    nc = bass.Bass("TRN2", target_bir_lowering=False, debug=False)
    DT = lambda n, s, d, k: nc.dram_tensor(n, s, d, kind=k).ap()
    xres2 = DT("xres2", [D, T], F32, "ExternalInput")
    WoB = DT("WoB", [D, D], F32, "ExternalInput")
    akall = DT("akall", [8, 64, 1024], F32, "ExternalInput")
    avall = DT("avall", [8, 64, 2048], F32, "ExternalInput")
    srall = DT("srall", [8, 1, 1024], F32, "ExternalInput")
    gnw = DT("gnw", [128, 2048], F32, "ExternalInput")
    cbigB = DT("cbigB", [128, 2048], F32, "ExternalInput")
    gwT_r = DT("gwT_r", [128, 128], F32, "ExternalInput")
    w1T = DT("w1T", [E_EXP, D, HID], F32, "ExternalInput")
    w3T = DT("w3T", [E_EXP, D, HID], F32, "ExternalInput")
    w2T = DT("w2T", [E_EXP, HID, D], F32, "ExternalInput")
    sw1T = DT("sw1T", [D, HID], F32, "ExternalInput")
    sw3T = DT("sw3T", [D, HID], F32, "ExternalInput")
    sw2T = DT("sw2T", [HID, D], F32, "ExternalInput")
    csm = DT("csm", [128, 960], F32, "ExternalInput")
    cbf = DT("cbf", [128, 1280], BF16, "ExternalInput")
    bq = DT("bq", [8, 128, T], BF16, "ExternalInput")
    bkfm = DT("bkfm", [8, 128, T], BF16, "ExternalInput")
    bktm = DT("bktm", [16, 128, HID], BF16, "ExternalInput")
    bE = DT("bE", [16, 128, HID], BF16, "ExternalInput")
    bv = DT("bv", [16, 128, D], BF16, "ExternalInput")
    bg = DT("bg", [16, 128, D], BF16, "ExternalInput")
    yT = DT("yT", [D, T], F32, "ExternalOutput")
    bo = DT("bo", [16, 128, D], BF16, "Internal")

    es = ExitStack()
    with es:
        p = Prog(nc)
        c = make_ctx(nc, p, es)
        emit_init(c, csm, cbf)
        if stop_after != "proj":
            emit_combine_b(c, akall, avall, srall)
            emit_scan_b(c, bq, bkfm, bktm, bE, bv, bg, cbigB, gnw, bo)
        if stop_after in ("wo", "ln4", "full"):
            emit_wo(c, WoB, bo, xres2, c.misc["d_x2"], yT, c.misc["d_yT"])
        if stop_after in ("ln4", "full"):
            emit_ln(c, yT, "l3b", xsrc_b=c.misc["d_yT"])
        if stop_after == "full":
            emit_moe(c, gwT_r, w1T, w3T, w2T, sw1T, sw3T, sw2T, yT,
                     c.misc["d_yT"], tag="m2")
        p.finalize(es)
    return nc


def build_consts_small(core_id):
    gam = _gammas()
    sm = np.zeros((128, 960), np.float32)
    sm[:, SM_ONESF] = 1.0
    sm[0, SM_ONESR:SM_ONESR + 128] = 1.0
    for h in range(A_HEADS):
        for j in range(8):
            w = gam[h] ** (2048.0 * (core_id - 1 - j)) if j < core_id else 0.0
            sm[:, SM_AW + h * 8 + j] = w
    for j in range(8):
        sm[:, SM_AW01 + j] = 1.0 if j < core_id else 0.0
    for h in range(A_HEADS):
        for sc in range(2):
            pos = sc * 128 + np.arange(128)
            sm[:, SM_KVSC + h * 2 + sc] = gam[h] ** (255.0 - pos)
        for tt in range(16):
            pos = tt * 128 + np.arange(128)
            sm[:, SM_KVSF + h * 16 + tt] = gam[h] ** (2047.0 - pos)
    return sm


def build_cbf():
    cb = np.zeros((128, 1280), BF)
    cb[:, 0:128] = np.eye(128)
    for e in range(8):
        cb[e, 128 + e * 128:128 + (e + 1) * 128] = 1.0
    idx = np.arange(128)
    cb[:, 1152:1280] = (idx[:, None] <= idx[None, :]).astype(np.float32)
    return cb


def _f32c(a):
    return np.ascontiguousarray(np.asarray(a), dtype=np.float32)


def _pad_w17(W, b):
    Wp = np.zeros((2176, W.shape[1]), np.float32)
    Wp[:2048] = W
    Wp[2048] = b
    return Wp


def build_cbigA():
    gam = _gammas()
    idx = np.arange(256)
    cb = np.zeros((128, 6144), np.float32)
    for h in range(A_HEADS):
        m = np.where(idx[None, :] >= idx[:, None],
                     gam[h] ** (idx[None, :] - idx[:, None]), 0.0)
        for sc in range(2):
            cb[:, h * 512 + sc * 256:h * 512 + sc * 256 + 256] = m[sc * 128:(sc + 1) * 128, :]
        cb[:, 4096 + h * 256:4096 + h * 256 + 256] = gam[h] ** (idx + 1)[None, :]
    return cb


def build_cbigB():
    idx = np.arange(128)
    m = (idx[:, None] <= idx[None, :]).astype(np.float32)
    return np.ascontiguousarray(np.tile(m, (1, 16)))


_NC_CACHE = {}


def _progs():
    if "l1" not in _NC_CACHE:
        _NC_CACHE["l1"] = build_l1()
        _NC_CACHE["l2"] = build_l2()
        _NC_CACHE["l3"] = build_l3()
    return _NC_CACHE["l1"], _NC_CACHE["l2"], _NC_CACHE["l3"]


def kernel(**inputs):
    I = inputs
    x = np.asarray(I["L0_attn_a_x"])[0]
    f32 = np.float32
    nc1, nc2, nc3 = _progs()

    xTs = [np.ascontiguousarray(x[c * T:(c + 1) * T].T).astype(f32) for c in range(NCORE)]
    WkA = _pad_w17(np.asarray(I["L0_attn_a_Wk"]), np.asarray(I["L0_attn_a_bk"]))
    WvA = _pad_w17(np.asarray(I["L0_attn_a_Wv"]), np.asarray(I["L0_attn_a_bv"]))
    WqA = _f32c(I["L0_attn_a_Wq"])
    cbf = build_cbf()
    csms = []
    for c in range(NCORE):
        sm = build_consts_small(c)
        sm[:, SM_BIASQ:SM_BIASQ + 16] = np.asarray(I["L0_attn_a_bq"]).reshape(16, 128).T
        sm[:, SM_BIASK:SM_BIASK + 16] = np.asarray(I["L0_attn_a_bk"]).reshape(16, 128).T
        csms.append(sm)

    in1 = [{"xT": xTs[c], "Wk": WkA, "Wv": WvA, "csm": csms[c], "cbf": cbf}
           for c in range(NCORE)]
    res1 = run_bass_kernel_spmd(nc1, in1, core_ids=list(range(NCORE)))
    r1 = res1.results
    sall = np.stack([np.asarray(r1[c]["sloc"]) for c in range(NCORE)]).astype(f32)

    gw = np.asarray(I["L0_ffn_gate_w"])
    gwT_r = np.ascontiguousarray(
        gw.T.reshape(16, 128, 8).transpose(1, 0, 2).reshape(128, 128)).astype(f32)
    moe_common = {
        "gwT_r": gwT_r,
        "w1T": np.ascontiguousarray(np.asarray(I["L0_ffn_w1"]).transpose(0, 2, 1)).astype(f32),
        "w3T": np.ascontiguousarray(np.asarray(I["L0_ffn_w3"]).transpose(0, 2, 1)).astype(f32),
        "w2T": np.ascontiguousarray(np.asarray(I["L0_ffn_w2"]).transpose(0, 2, 1)).astype(f32),
        "sw1T": np.ascontiguousarray(np.asarray(I["L0_ffn_sw1"]).T).astype(f32),
        "sw3T": np.ascontiguousarray(np.asarray(I["L0_ffn_sw3"]).T).astype(f32),
        "sw2T": np.ascontiguousarray(np.asarray(I["L0_ffn_sw2"]).T).astype(f32),
    }
    cbigA = build_cbigA()
    in2 = [dict(moe_common, **{
        "xT": xTs[c], "xn1": np.asarray(r1[c]["xn1"]),
        "aktm": np.asarray(r1[c]["aktm"]), "av": np.asarray(r1[c]["av"]),
        "Wq": WqA, "Wk": WkA, "sall": sall, "cbigA": cbigA,
        "WqB": _f32c(I["L0_attn_b_Wq"]), "WkB": _f32c(I["L0_attn_b_Wk"]),
        "WsB": _f32c(I["L0_attn_b_Ws"]), "WvB": _f32c(I["L0_attn_b_Wv"]),
        "WgB": _f32c(I["L0_attn_b_Wg"]),
        "csm": csms[c], "cbf": cbf}) for c in range(NCORE)]
    res2 = run_bass_kernel_spmd(nc2, in2, core_ids=list(range(NCORE)))
    r2 = res2.results
    akall = np.stack([np.asarray(r2[c]["akb"]) for c in range(NCORE)]).astype(f32)
    avall = np.stack([np.asarray(r2[c]["avb"]) for c in range(NCORE)]).astype(f32)
    srall = np.stack([np.asarray(r2[c]["srb"]) for c in range(NCORE)]).astype(f32)

    gnw = np.asarray(I["L0_attn_b_g_norm_weight"])
    gnw_bc = np.ascontiguousarray(
        np.broadcast_to(np.tile(gnw, 16)[None, :], (128, 2048))).astype(f32)
    in3 = [dict(moe_common, **{
        "xres2": np.asarray(r2[c]["xres"]).astype(f32),
        "bq": np.asarray(r2[c]["bq"]), "bkfm": np.asarray(r2[c]["bkfm"]),
        "bktm": np.asarray(r2[c]["bktm"]), "bE": np.asarray(r2[c]["bE"]),
        "bv": np.asarray(r2[c]["bv"]), "bg": np.asarray(r2[c]["bg"]),
        "WoB": _f32c(I["L0_attn_b_Wo"]),
        "akall": akall, "avall": avall, "srall": srall,
        "gnw": gnw_bc, "cbigB": build_cbigB(),
        "csm": csms[c], "cbf": cbf}) for c in range(NCORE)]
    res3 = run_bass_kernel_spmd(nc3, in3, core_ids=list(range(NCORE)))

    out = np.empty((1, 16384, 2048), np.float32)
    for c in range(NCORE):
        out[0, c * T:(c + 1) * T, :] = np.asarray(res3.results[c]["yT"]).T
    return out
